# version 22
# baseline (speedup 1.0000x reference)
import numpy as np
import concourse.bass as bass
import concourse.mybir as mybir
from concourse.bass_utils import run_bass_kernel_spmd

F32 = mybir.dt.float32
BF16 = mybir.dt.bfloat16
I32 = mybir.dt.int32
AF = mybir.ActivationFunctionType
ALU = mybir.AluOpType
AXX = mybir.AxisListType.X

T_SEQ = 4096
T_TILE = 256
D = 1024
D_INNER = 2048
CONV_DIM = 2304
D_ZX = 4384
H_SSM = 32
HIDDEN = 3072

EPS = float(np.float32(1e-5))
C127_HI = float(np.float32(1.0 / 127.0))
C127_LO = float(np.float32(1.0 / 127.0 - np.float64(np.float32(1.0 / 127.0))))
C2 = 1.0 / (127.0 * 127.0)
C2_HI = float(np.float32(C2))
C2_LO = float(np.float32(C2 - np.float64(np.float32(C2))))

ENGINES = ("sync", "gpsimd", "vector", "scalar", "tensor")


def _rope_tables():
    import jax
    import jax.numpy as jnp
    cpu = jax.devices("cpu")[0]
    with jax.default_device(cpu):
        inv_freq = 1.0 / 10000.0 ** (jnp.arange(0, 64, 2, dtype=jnp.float32) / 64)
        freqs = jnp.arange(T_SEQ, dtype=jnp.float32)[:, None] * inv_freq[None, :]
        cos = np.asarray(jnp.cos(freqs), dtype=np.float32)
        sin = np.asarray(jnp.sin(freqs), dtype=np.float32)
    return np.ascontiguousarray(cos), np.ascontiguousarray(sin)


class Em:

    def __init__(self):
        self.streams = {e: [] for e in ENGINES}
        self.planned = {}
        self.sem_names = []

    def sem(self, name):
        if name not in self.planned:
            self.planned[name] = 0
            self.sem_names.append(name)
        return name

    def op(self, eng, fn):
        self.streams[eng].append({"fn": fn, "incs": []})

    def inc(self, eng, sem, n=1):
        self.sem(sem)
        assert self.streams[eng] and "fn" in self.streams[eng][-1]
        for (s0, _) in self.streams[eng][-1]["incs"]:
            if s0 == sem:
                return self.planned[sem]
        self.streams[eng][-1]["incs"].append((sem, n))
        self.planned[sem] += n
        return self.planned[sem]

    def wait(self, eng, sem, val):
        self.sem(sem)
        if val > 0:
            self.streams[eng].append({"wait": (sem, val)})

    def cur(self, sem):
        return self.planned.get(sem, 0)


def build_program(n_tiles=16, debug_taps=()):
    nc = bass.Bass(target_bir_lowering=False, trn_type="TRN2")

    x_in = nc.dram_tensor("x", [T_SEQ, D], F32, kind="ExternalInput")
    out_d = nc.dram_tensor("out", [T_SEQ, D], F32, kind="ExternalOutput")

    wq_td = nc.dram_tensor("wq_t", [D, D], F32, kind="ExternalInput")
    wz_td = nc.dram_tensor("wz_t", [D, D_ZX], F32, kind="ExternalInput")
    wo_td = nc.dram_tensor("wo_t", [D_INNER, D], F32, kind="ExternalInput")
    wg_td = nc.dram_tensor("wg_t", [D, HIDDEN], I32, kind="ExternalInput")
    wu_td = nc.dram_tensor("wu_t", [D, HIDDEN], I32, kind="ExternalInput")
    wd_td = nc.dram_tensor("wd_t", [HIDDEN, D], I32, kind="ExternalInput")
    convw_d = nc.dram_tensor("conv_w", [128, 72], F32, kind="ExternalInput")
    convb_d = nc.dram_tensor("conv_b", [128, 18], F32, kind="ExternalInput")
    dtb_d = nc.dram_tensor("dt_bias", [H_SSM, 1], F32, kind="ExternalInput")
    alog_d = nc.dram_tensor("a_log", [1, H_SSM], F32, kind="ExternalInput")
    dpar_d = nc.dram_tensor("d_par", [1, H_SSM], F32, kind="ExternalInput")
    npw_d = nc.dram_tensor("npw", [128, 8], F32, kind="ExternalInput")
    ngw_d = nc.dram_tensor("ngw", [128, 16], F32, kind="ExternalInput")
    triu_d = nc.dram_tensor("triu", [128, 128], F32, kind="ExternalInput")
    ident_d = nc.dram_tensor("ident", [128, 128], F32, kind="ExternalInput")
    onesc_d = nc.dram_tensor("ones_c", [128, 1], F32, kind="ExternalInput")
    ones_d = nc.dram_tensor("ones_r", [1, 128], F32, kind="ExternalInput")
    cos_d = nc.dram_tensor("cos_t", [T_SEQ, 32], F32, kind="ExternalInput")
    sin_d = nc.dram_tensor("sin_t", [T_SEQ, 32], F32, kind="ExternalInput")

    wg_b_d = nc.dram_tensor("wg_b", [D, HIDDEN], BF16, kind="Internal")
    wu_b_d = nc.dram_tensor("wu_b", [D, HIDDEN], BF16, kind="Internal")
    wd_b_d = nc.dram_tensor("wd_b", [HIDDEN, D], BF16, kind="Internal")
    wq_hi_d = nc.dram_tensor("wq_hi", [D, D], BF16, kind="Internal")
    wq_lo_d = nc.dram_tensor("wq_lo", [D, D], BF16, kind="Internal")
    wz_hi_d = nc.dram_tensor("wz_hi", [D, D_ZX], BF16, kind="Internal")
    wz_lo_d = nc.dram_tensor("wz_lo", [D, D_ZX], BF16, kind="Internal")
    wo_hi_d = nc.dram_tensor("wo_hi", [D_INNER, D], BF16, kind="Internal")
    wo_lo_d = nc.dram_tensor("wo_lo", [D_INNER, D], BF16, kind="Internal")

    dbg_ds = [nc.dram_tensor(f"dbg{i}", list(shape), F32, kind="ExternalOutput")
              for i, (_, _, shape) in enumerate(debug_taps)]

    A = nc.alloc_sbuf_tensor
    triu = A("c_triu", [128, 128], F32)
    ident = A("c_ident", [128, 128], F32)
    ones_c = A("c_onesc", [128, 1], F32)
    ones_r = A("c_onesr", [1, 128], F32)
    cosb = A("c_cos", [128, 64], F32)
    sinb = A("c_sin", [128, 64], F32)
    abc = A("c_abc", [128, 32], F32)
    abcn = A("c_abcn", [128, 32], F32)
    dbc = A("c_dbc", [128, 32], F32)
    arow = A("c_arow", [1, 32], F32)
    drow = A("c_drow", [1, 32], F32)
    dtbc = A("c_dtb", [32, 1], F32)
    convw = A("c_convw", [128, 72], F32)
    convb = A("c_convb", [128, 18], F32)
    npw = A("c_npw", [128, 8], F32)
    ngw = A("c_ngw", [128, 16], F32)
    halo = A("b_halo", [128, 54], F32)
    epsc = A("c_eps", [128, 1], F32)

    dt_f = A("b_dtf", [32, 256], F32)
    dt_e = A("b_dte", [32, 256], F32)
    dtT = A("b_dtT", [128, 64], F32)
    adt = A("b_adt", [128, 64], F32)
    adtn = A("b_adtn", [128, 64], F32)
    acst = A("b_acst", [128, 64], F32)
    tmp32 = A("b_tmp32", [128, 64], F32)
    ddt = A("b_ddt", [128, 64], F32)
    eacst = A("b_eacst", [128, 64], F32)
    eab = A("b_eab", [128, 64], F32)
    dtd = A("b_dtd", [128, 64], F32)
    stats = A("b_stats", [128, 16], F32)
    esb = A("b_esb", [128, 512], F32)
    mtsb = A("b_mtsb", [128, 512], F32)
    gtm = A("b_gtm", [128, 128], F32)
    btsb = A("b_btsb", [128, 128], F32)

    qt = A("b_qt", [128, 2048], F32)
    xnh = A("b_xnh", [128, 2048], BF16)
    xnl = A("b_xnl", [128, 2048], BF16)
    hh_ = A("b_hh", [128, 2048], BF16)
    hl_ = A("b_hl", [128, 2048], BF16)
    zt = A("b_zt", [128, 4096], F32)
    pf1 = A("b_pf1", [128, 4662], F32)
    pf2 = A("b_pf2", [128, 4608], F32)
    pf3 = A("b_pf3", [128, 4096], F32)
    pf7 = A("b_pf7", [128, 2048], F32)
    xts = A("b_xts", [128, 2048], F32)
    xdt = A("b_xdt", [128, 2048], F32)
    sxdt = A("b_sxdt", [128, 2048], F32)
    r_st = A("b_r", [128, 2048], F32)
    ghl = A("b_ghl", [128, 8192], BF16)
    esb2 = A("b_esb2", [128, 512], F32)
    mtsb2 = A("b_mtsb2", [128, 512], F32)
    yevt = A("b_yevt", [128, 512], F32)
    outt = A("b_out", [128, 2048], F32)
    wbh = A("b_wbh", [128, 4096], BF16)
    wbl = A("b_wbl", [128, 4096], BF16)
    wbh2 = A("b_wbh2", [128, 4096], BF16)
    wbl2 = A("b_wbl2", [128, 4096], BF16)

    pb = [nc.alloc_psum_tensor(f"pb{i}", [128, 512], F32) for i in range(8)]


    em = Em()
    S = {e: f"S_{e}" for e in ENGINES}
    DQ = "S_dma_sync"
    DG = "S_dma_gps"
    DC_ = "S_dma_const"
    DA = "S_dma_aux"
    DW = ["S_dma_w0", "S_dma_w1"]
    DX = "S_dma_x"

    def dma(eng_q, sem, out, in_):
        em.op(eng_q, lambda e, o=out, i=in_: e.dma_start(out=o, in_=i))
        return em.inc(eng_q, sem, 16)

    def fence(eng):
        return em.inc(eng, S[eng], 1)

    dma("sync", DC_, triu[:], triu_d.ap())
    dma("sync", DC_, ident[:], ident_d.ap())
    dma("sync", DC_, ones_c[:], onesc_d.ap())
    dma("sync", DC_, ones_r[:], ones_d.ap())
    dma("sync", DC_, arow[:], alog_d.ap())
    dma("sync", DC_, drow[:], dpar_d.ap())
    dma("sync", DC_, dtbc[:], dtb_d.ap())
    dma("sync", DC_, convw[:], convw_d.ap())
    dma("sync", DC_, convb[:], convb_d.ap())
    dma("sync", DC_, npw[:], npw_d.ap())
    ld0 = dma("sync", DC_, ngw[:], ngw_d.ap())

    em.wait("scalar", DC_, ld0)
    em.op("scalar", lambda e: e.activation(arow[:], arow[:], AF.Exp))
    fs = fence("scalar")
    em.wait("vector", S["scalar"], fs)
    em.op("vector", lambda e: e.tensor_scalar_mul(arow[:], arow[:], -1.0))
    em.op("vector", lambda e: e.memset(halo[:], 0.0))
    em.op("vector", lambda e: e.memset(epsc[:], EPS))
    em.op("vector", lambda e: e.memset(r_st[:], 0.0))
    fv = fence("vector")
    em.wait("tensor", S["vector"], fv)
    em.op("tensor", lambda e: e.matmul(pb[0][:, 0:32], ones_r[:], arow[:],
                                       start=True, stop=True))
    em.op("tensor", lambda e: e.matmul(pb[0][:, 32:64], ones_r[:], drow[:],
                                       start=True, stop=True))
    ft = fence("tensor")
    em.wait("vector", S["tensor"], ft)
    em.op("vector", lambda e: e.tensor_copy(abc[:], pb[0][:, 0:32]))
    em.op("vector", lambda e: e.tensor_copy(dbc[:], pb[0][:, 32:64]))
    em.op("vector", lambda e: e.tensor_scalar(abcn[:], pb[0][:, 0:32], -1.0, None, ALU.mult))
    fence("vector")

    dsem = ["S_dma_prepg0", "S_dma_prepg1"]

    def ghq(pi, w, lo=False):
        return bass.AP(ghl, pi * 2048 + (4096 if lo else 0),
                       [[8192, 128], [1, w]])

    def prep_fp32(src_d, hi_d, lo_d, K, Dout):
        pieces = [(i * 2048, min(2048, Dout - i * 2048))
                  for i in range((Dout + 2047) // 2048)]
        stage = [pf1, pf2]
        ssem = ["S_dma_prep0", "S_dma_prep1"]
        fv_done = {}
        pi_ctr = [0]
        st_tok = {0: 0, 1: 0}
        for k in range(K // 128):
            sb = stage[k % 2]
            if k >= 2:
                em.wait("sync", S["vector"], fv_done[k - 2])
            t = dma("sync", ssem[k % 2], sb[:, 0:Dout],
                    bass.AP(src_d, k * 128 * Dout, [[Dout, 128], [1, Dout]]))
            em.wait("scalar", ssem[k % 2], t)
            for (c0, w) in pieces:
                pi = pi_ctr[0] % 2
                pi_ctr[0] += 1
                em.wait("scalar", dsem[pi], st_tok[pi])
                em.op("scalar", lambda e, sb=sb, c0=c0, w=w, pi=pi: e.copy(
                    ghq(pi, w), sb[:, c0:c0 + w]))
                fs2 = fence("scalar")
                em.wait("vector", S["scalar"], fs2)
                em.wait("vector", dsem[pi], st_tok[pi])
                em.op("vector", lambda e, sb=sb, c0=c0, w=w, pi=pi: e.scalar_tensor_tensor(
                    ghq(pi, w, True), sb[:, c0:c0 + w], 1.0, ghq(pi, w),
                    ALU.mult, ALU.subtract))
                fv2 = fence("vector")
                em.wait("gpsimd", S["vector"], fv2)
                dma("gpsimd", dsem[pi],
                    bass.AP(hi_d, k * 128 * Dout + c0, [[Dout, 128], [1, w]]),
                    ghq(pi, w))
                st_tok[pi] = dma("gpsimd", dsem[pi],
                                 bass.AP(lo_d, k * 128 * Dout + c0,
                                         [[Dout, 128], [1, w]]),
                                 ghq(pi, w, True))
            fv_done[k] = fv2
        for pi in range(2):
            em.wait("sync", dsem[pi], st_tok[pi])
            em.wait("gpsimd", dsem[pi], st_tok[pi])
        return dict(st_tok)

    DGI = [DG, "S_dma_int1"]
    DGL = "S_dma_gl"
    int_last = {0: None, 1: None}
    int_tok = [0, 0]

    def gint(pi, w, second=False):
        return bass.AP(ghl, (4096 + pi * 1024) if second else pi * 2048,
                       [[8192, 128], [1, w]])

    def prep_int(src_d, dst_d, K, Dout):
        w1 = min(2048, Dout)
        w2 = Dout - w1
        for k in range(K // 128):
            pi = k % 2
            if int_last[pi] is not None:
                em.wait("gpsimd", DGI[pi], int_last[pi])
            t = dma("gpsimd", DGL, gint(pi, w1),
                    bass.AP(src_d, k * 128 * Dout, [[Dout, 128], [1, w1]]))
            if w2:
                t = dma("gpsimd", DGL, gint(pi, w2, True),
                        bass.AP(src_d, k * 128 * Dout + w1,
                                [[Dout, 128], [1, w2]]))
            em.wait("gpsimd", DGL, t)
            t2 = dma("gpsimd", DGI[pi],
                     bass.AP(dst_d, k * 128 * Dout, [[Dout, 128], [1, w1]]),
                     gint(pi, w1))
            if w2:
                t2 = dma("gpsimd", DGI[pi],
                         bass.AP(dst_d, k * 128 * Dout + w1,
                                 [[Dout, 128], [1, w2]]),
                         gint(pi, w2, True))
            int_last[pi] = t2
            int_tok[pi] = t2

    wpair = [{"ht": wbh, "hb": 0, "lt": wbl, "lb": 0, "stride": 4096,
              "free": 0, "sem": DW[0], "gsem": "S_dma_wg0"},
             {"ht": wbh2, "hb": 0, "lt": wbl2, "lb": 0, "stride": 4096,
              "free": 0, "sem": DW[1], "gsem": "S_dma_wg1"},
             {"ht": ghl, "hb": 0, "lt": ghl, "lb": 4096, "stride": 8192,
              "free": 0, "sem": "S_dma_w2", "gsem": "S_dma_wg2"}]
    widx = [0]

    def wap(p, which, off, rest):
        return bass.AP(p[which + "t"], p[which + "b"] + off,
                       [[p["stride"], 128]] + rest)

    def wload(specs, eng="sync", n=2):
        p = wpair[widx[0] % n]
        widx[0] += 1
        em.wait(eng, S["tensor"], p["free"])
        sem = p["sem"] if eng == "sync" else p["gsem"]
        tok = 0
        for (which, src_d, Dout, col0, width, kchunks, k0) in specs:
            em.op(eng, lambda e, p=p, which=which, src=src_d, Do=Dout,
                  c0=col0, w=width, kc=kchunks, kz=k0: e.dma_start(
                      out=wap(p, which, 0, [[w, kc], [1, w]]),
                      in_=bass.AP(src, (kz * 128) * Do + c0,
                                  [[Do, 128], [128 * Do, kc], [1, w]])))
            tok = em.inc(eng, sem, 16)
        return p, sem, tok

    def wfree(p, ftok):
        p["free"] = ftok

    trope_box = [0]

    def t0_load_ln(tile2):
        trow2 = tile2 * T_TILE
        em.wait("gpsimd", S["vector"], em.cur(S["vector"]))
        tx = None
        for c in range(2):
            tx = dma("gpsimd", DX, pf3[:, c * 1024:(c + 1) * 1024],
                     bass.AP(x_in, (trow2 + c * 128) * D, [[D, 128], [1, D]]))
        dma("sync", DA, bass.AP(cosb, 0, [[64, 128], [32, 2], [1, 32]]),
            bass.AP(cos_d, trow2 * 32, [[32, 128], [32 * 128, 2], [1, 32]]))
        trope2 = dma("sync", DA, bass.AP(sinb, 0, [[64, 128], [32, 2], [1, 32]]),
                     bass.AP(sin_d, trow2 * 32, [[32, 128], [32 * 128, 2], [1, 32]]))
        trope_box[0] = trope2
        em.wait("vector", DX, tx)
        em.op("vector", lambda e: e.tensor_reduce(
            bass.AP(stats, 0, [[16, 128], [1, 2]]),
            bass.AP(pf3, 0, [[4096, 128], [1024, 2], [1, 1024]]), AXX, ALU.add))
        fv = fence("vector")
        em.wait("scalar", S["vector"], fv)
        em.op("scalar", lambda e: e.activation(stats[:, 8:10], stats[:, 0:2],
                                               AF.Identity, scale=1.0 / 1024.0))
        fs = fence("scalar")
        em.wait("vector", S["scalar"], fs)
        for c in range(2):
            em.op("vector", lambda e, c=c: e.tensor_scalar(
                pf3[:, c * 1024:(c + 1) * 1024], pf3[:, c * 1024:(c + 1) * 1024],
                stats[:, 8 + c:9 + c], None, ALU.subtract))
        _sf = fence("vector")
        em.wait("vector", S["vector"], _sf)
        for c in range(2):
            em.op("vector", lambda e, c=c: e.tensor_tensor(
                pf7[:, c * 1024:(c + 1) * 1024], pf3[:, c * 1024:(c + 1) * 1024],
                pf3[:, c * 1024:(c + 1) * 1024], ALU.mult))
        _sf = fence("vector")
        em.wait("vector", S["vector"], _sf)
        for c in range(2):
            em.op("vector", lambda e, c=c: e.tensor_reduce(
                bass.AP(stats, 2 + c, [[16, 128], [1, 1]]),
                bass.AP(pf7, c * 1024, [[2048, 128], [1, 1024]]), AXX, ALU.add))
        fv = fence("vector")
        em.wait("scalar", S["vector"], fv)
        em.op("scalar", lambda e: e.activation(stats[:, 4:6], stats[:, 2:4], AF.Ln,
                                               bias=epsc[:, 0:1], scale=1.0 / 1024.0))
        _sfs = fence("scalar")
        em.wait("scalar", S["scalar"], _sfs)
        em.op("scalar", lambda e: e.activation(stats[:, 6:8], stats[:, 4:6], AF.Exp,
                                               scale=-0.5))
        fs = fence("scalar")
        em.wait("vector", S["scalar"], fs)
        for c in range(2):
            em.op("vector", lambda e, c=c: e.tensor_scalar(
                pf3[:, c * 1024:(c + 1) * 1024], pf3[:, c * 1024:(c + 1) * 1024],
                stats[:, 6 + c:7 + c], None, ALU.mult))
        return fence("vector")

    fv_ln0 = t0_load_ln(0)
    prep_fp32(wq_td, wq_hi_d, wq_lo_d, D, D)
    preload_q = [wload([("h", wq_hi_d, D, so * 512, 512, 8, 0),
                        ("l", wq_lo_d, D, so * 512, 512, 8, 0)])
                 for so in range(2)]
    prep_fp32(wz_td, wz_hi_d, wz_lo_d, D, D_ZX)
    prep_fp32(wo_td, wo_hi_d, wo_lo_d, D_INNER, D)
    em.wait("gpsimd", S["scalar"], em.cur(S["scalar"]))
    em.wait("gpsimd", S["vector"], em.cur(S["vector"]))
    prep_int(wg_td, wg_b_d, D, HIDDEN)
    prep_int(wu_td, wu_b_d, D, HIDDEN)
    prep_int(wd_td, wd_b_d, HIDDEN, D)
    em.wait("gpsimd", DGI[0], int_tok[0])
    em.wait("gpsimd", DGI[1], int_tok[1])

    tapmap = {}
    for i, (phase, buf, shape) in enumerate(debug_taps):
        tapmap.setdefault(phase, []).append((i, buf))

    def do_taps(phase, tile, locs):
        if tile != 0 or phase not in tapmap:
            return
        for i, buf in tapmap[phase]:
            em.wait("gpsimd", S["vector"], em.cur(S["vector"]))
            em.wait("gpsimd", S["scalar"], em.cur(S["scalar"]))
            em.wait("gpsimd", S["tensor"], em.cur(S["tensor"]))
            em.wait("gpsimd", DG, em.cur(DG))
            em.wait("gpsimd", DX, em.cur(DX))
            em.wait("gpsimd", DQ, em.cur(DQ))
            em.wait("gpsimd", DC_, em.cur(DC_))
            em.wait("gpsimd", DA, em.cur(DA))
            t = dma("gpsimd", DX, dbg_ds[i].ap(), locs[buf])
            em.wait("gpsimd", DX, t)
            em.wait("vector", DX, t)
            em.wait("scalar", DX, t)
            em.wait("tensor", DX, t)
            em.wait("sync", DX, t)

    def transpose_1024_block(src, evac):
        for rnd in range(2):
            for kk in range(4):
                k = rnd * 4 + kk
                for c in range(2):
                    em.op("tensor", lambda e, k=k, c=c, kk=kk: e.transpose(
                        pb[kk][:, c * 128:(c + 1) * 128],
                        src[:, c * 1024 + k * 128: c * 1024 + (k + 1) * 128],
                        ident[:]))
            ftt = fence("tensor")
            em.wait("scalar", S["tensor"], ftt)
            for kk in range(4):
                evac(rnd * 4 + kk, kk, 0)
            fss = fence("scalar")
            em.wait("vector", S["scalar"], fss)
            for kk in range(4):
                evac(rnd * 4 + kk, kk, 1)
            fvv = fence("vector")
            em.wait("tensor", S["vector"], fvv)
        return fvv

    fv_ln = 0
    for tile in range(n_tiles):
        trow = tile * T_TILE


        if tile == 0:
            fv_ln = fv_ln0
        fv = fv_ln
        trope = trope_box[0]
        do_taps("ln", tile, {"xn_t": pf3[:, 0:2048]})
        em.wait("tensor", S["vector"], fv)

        def evac_xn(k, kk, phase):
            if phase == 0:
                em.op("scalar", lambda e, k=k, kk=kk: e.copy(
                    xnh[:, k * 256:(k + 1) * 256], pb[kk][:, 0:256]))
            else:
                em.op("vector", lambda e, k=k, kk=kk: e.scalar_tensor_tensor(
                    xnl[:, k * 256:(k + 1) * 256], pb[kk][:, 0:256], 1.0,
                    xnh[:, k * 256:(k + 1) * 256], ALU.mult, ALU.subtract))
        transpose_1024_block(pf3, evac_xn)

        for so in range(2):
            if tile == 0:
                p, wsem, wt = preload_q[so]
            else:
                p, wsem, wt = wload([("h", wq_hi_d, D, so * 512, 512, 8, 0),
                               ("l", wq_lo_d, D, so * 512, 512, 8, 0)])
            em.wait("tensor", wsem, wt)
            for c in range(2):
                bank = pb[4 + 2 * so + c]
                for k in range(8):
                    ah = xnh[:, k * 256 + c * 128: k * 256 + (c + 1) * 128]
                    al = xnl[:, k * 256 + c * 128: k * 256 + (c + 1) * 128]
                    wh = wap(p, "h", k * 512, [[1, 512]])
                    wl = wap(p, "l", k * 512, [[1, 512]])
                    em.op("tensor", lambda e, b=bank, a=ah, w=wh, k=k: e.matmul(
                        b[:], a, w, start=(k == 0), stop=False))
                    em.op("tensor", lambda e, b=bank, a=al, w=wh: e.matmul(
                        b[:], a, w, start=False, stop=False))
                    em.op("tensor", lambda e, b=bank, a=ah, w=wl, k=k: e.matmul(
                        b[:], a, w, start=False, stop=(k == 7)))
            ftt = fence("tensor")
            wfree(p, ftt)
        em.wait("scalar", S["tensor"], ftt)
        for so in range(2):
            for c in range(2):
                em.op("scalar", lambda e, so=so, c=c: e.copy(
                    pf7[:, c * 1024 + so * 512: c * 1024 + (so + 1) * 512],
                    pb[4 + 2 * so + c][:]))
        fs_q = fence("scalar")
        em.wait("vector", S["scalar"], fs_q)
        em.wait("vector", DA, trope)
        combines = []
        for c in range(2):
            base = c * 1024
            qr_lo = bass.AP(pf7, base, [[2048, 128], [64, 16], [1, 32]])
            qr_hi = bass.AP(pf7, base + 32, [[2048, 128], [64, 16], [1, 32]])
            cos_ap = bass.AP(cosb, c * 32, [[64, 128], [0, 16], [1, 32]])
            sin_ap = bass.AP(sinb, c * 32, [[64, 128], [0, 16], [1, 32]])
            qt_lo = bass.AP(qt, base, [[2048, 128], [64, 16], [1, 32]])
            qt_hi = bass.AP(qt, base + 32, [[2048, 128], [64, 16], [1, 32]])
            tbuf = xdt if c == 0 else sxdt
            t_ap = lambda j, tb=tbuf: bass.AP(tb, j * 512, [[2048, 128], [32, 16], [1, 32]])
            em.op("vector", lambda e, a=qr_lo, b=cos_ap, o=t_ap(0): e.tensor_tensor(o, a, b, ALU.mult))
            em.op("vector", lambda e, a=qr_hi, b=sin_ap, o=t_ap(1): e.tensor_tensor(o, a, b, ALU.mult))
            em.op("vector", lambda e, a=qr_hi, b=cos_ap, o=t_ap(2): e.tensor_tensor(o, a, b, ALU.mult))
            em.op("vector", lambda e, a=qr_lo, b=sin_ap, o=t_ap(3): e.tensor_tensor(o, a, b, ALU.mult))
            combines.append((qt_lo, t_ap(0), t_ap(1), ALU.subtract))
            combines.append((qt_hi, t_ap(2), t_ap(3), ALU.add))
        _sf = fence("vector")
        em.wait("vector", S["vector"], _sf)
        for (o, a, b, op_) in combines:
            em.op("vector", lambda e, o=o, a=a, b=b, op_=op_: e.tensor_tensor(o, a, b, op_))
        fv = fence("vector")
        do_taps("rope", tile, {"qt": qt[:]})

        _sf = fence("vector")
        em.wait("vector", S["vector"], _sf)
        for c in range(2):
            em.op("vector", lambda e, c=c: e.tensor_tensor(
                pf7[:, c * 1024:(c + 1) * 1024], qt[:, c * 1024:(c + 1) * 1024],
                qt[:, c * 1024:(c + 1) * 1024], ALU.mult))
        _sf = fence("vector")
        em.wait("vector", S["vector"], _sf)
        for c in range(2):
            em.op("vector", lambda e, c=c: e.tensor_reduce(
                bass.AP(stats, c, [[16, 128], [1, 1]]),
                bass.AP(pf7, c * 1024, [[2048, 128], [1, 1024]]), AXX, ALU.add))
        fv = fence("vector")
        em.wait("scalar", S["vector"], fv)
        em.op("scalar", lambda e: e.activation(stats[:, 4:6], stats[:, 0:2], AF.Ln,
                                               bias=epsc[:, 0:1], scale=1.0 / 1024.0))
        _sfs = fence("scalar")
        em.wait("scalar", S["scalar"], _sfs)
        em.op("scalar", lambda e: e.activation(stats[:, 6:8], stats[:, 4:6], AF.Exp,
                                               scale=-0.5))
        fs = fence("scalar")
        em.wait("vector", S["scalar"], fs)
        for c in range(2):
            em.op("vector", lambda e, c=c: e.tensor_scalar(
                pf7[:, c * 1024:(c + 1) * 1024], qt[:, c * 1024:(c + 1) * 1024],
                stats[:, 6 + c:7 + c], None, ALU.mult))
        fv = fence("vector")
        em.wait("tensor", S["vector"], fv)

        def evac_h(k, kk, phase):
            if phase == 0:
                em.op("scalar", lambda e, k=k, kk=kk: e.activation(
                    hh_[:, k * 256:(k + 1) * 256], pb[kk][:, 0:256],
                    AF.Identity, scale=npw[:, k:k + 1]))
            else:
                em.op("vector", lambda e, k=k, kk=kk: e.scalar_tensor_tensor(
                    hl_[:, k * 256:(k + 1) * 256], pb[kk][:, 0:256], npw[:, k:k + 1],
                    hh_[:, k * 256:(k + 1) * 256], ALU.mult, ALU.subtract))
        transpose_1024_block(pf7, evac_h)

        fv_evac = 0
        for sf in range(5):
            col0 = D_INNER + sf * 512
            width = min(512, D_ZX - col0)
            p, wsem, wt = wload([("h", wz_hi_d, D_ZX, col0, width, 8, 0),
                           ("l", wz_lo_d, D_ZX, col0, width, 8, 0)])
            em.wait("tensor", wsem, wt)
            if sf >= 1:
                em.wait("tensor", S["vector"], fv_evac)
            nm = (width + 127) // 128
            for m in range(nm):
                mw = min(128, width - m * 128)
                bank = pb[m % 4]
                for k in range(8):
                    wh = wap(p, "h", k * width + m * 128, [[1, mw]])
                    wl = wap(p, "l", k * width + m * 128, [[1, mw]])
                    rh = hh_[:, k * 256:(k + 1) * 256]
                    rl = hl_[:, k * 256:(k + 1) * 256]
                    em.op("tensor", lambda e, b=bank, w=wh, r=rh, k=k, mw=mw: e.matmul(
                        b[0:mw, 0:256], w, r, start=(k == 0), stop=False))
                    em.op("tensor", lambda e, b=bank, w=wl, r=rh, mw=mw: e.matmul(
                        b[0:mw, 0:256], w, r, start=False, stop=False))
                    em.op("tensor", lambda e, b=bank, w=wh, r=rl, k=k, mw=mw: e.matmul(
                        b[0:mw, 0:256], w, r, start=False, stop=(k == 7)))
            ftt = fence("tensor")
            wfree(p, ftt)
            em.wait("vector", S["tensor"], ftt)
            for m in range(nm):
                g = sf * 4 + m
                bank = pb[m % 4]
                if g < 18:
                    em.op("vector", lambda e, g=g, b=bank: e.tensor_copy(
                        pf1[:, g * 259 + 3: g * 259 + 259], b[:, 0:256]))
                else:
                    em.op("vector", lambda e, b=bank: e.tensor_copy(
                        dt_f[:, 0:256], b[0:32, 0:256]))
            fv_evac = fence("vector")
        em.wait("tensor", S["vector"], fv_evac)
        for g in range(18):
            em.op("vector", lambda e, g=g: e.tensor_copy(
                pf1[:, g * 259: g * 259 + 3], halo[:, g * 3:(g + 1) * 3]))
        fv = fence("vector")
        em.wait("scalar", S["vector"], fv)
        em.op("scalar", lambda e: e.activation(dt_e[:], dt_f[:], AF.Exp, bias=dtbc[:, 0:1]))
        fs = fence("scalar")
        em.wait("vector", S["scalar"], fs)
        em.op("vector", lambda e: e.tensor_scalar_add(dt_e[:], dt_e[:], 1.0))
        fv = fence("vector")
        em.wait("scalar", S["vector"], fv)
        em.op("scalar", lambda e: e.activation(dt_f[:], dt_e[:], AF.Ln))
        fs_dt = fence("scalar")
        em.wait("tensor", S["scalar"], fs_dt)
        for c in range(2):
            em.op("tensor", lambda e, c=c: e.transpose(
                pb[0][:, c * 32:(c + 1) * 32], dt_f[0:32, c * 128:(c + 1) * 128],
                ident[0:32, 0:32]))
        ftt = fence("tensor")
        em.wait("scalar", S["tensor"], ftt)
        em.op("scalar", lambda e: e.copy(dtT[:], pb[0][:, 0:64]))
        fs = fence("scalar")
        fs_dtT = fs
        em.wait("vector", S["scalar"], fs)
        em.op("vector", lambda e: e.tensor_tensor(
            adt[:], dtT[:], bass.AP(abc, 0, [[32, 128], [0, 2], [1, 32]]), ALU.mult))
        em.op("vector", lambda e: e.tensor_tensor(
            adtn[:], dtT[:], bass.AP(abcn, 0, [[32, 128], [0, 2], [1, 32]]), ALU.mult))
        fv_dt = fence("vector")
        em.wait("vector", S["vector"], fv_dt) if False else None
        for g in range(18):
            em.op("vector", lambda e, g=g: e.tensor_scalar(
                pf2[:, g * 256:(g + 1) * 256], pf1[:, g * 259: g * 259 + 256],
                convw[:, 4 * g:4 * g + 1], convb[:, g:g + 1],
                ALU.mult, ALU.add))
        for tap in (1, 2, 3):
            _sf = fence("vector")
            em.wait("vector", S["vector"], _sf)
            for g in range(18):
                em.op("vector", lambda e, g=g, t=tap: e.scalar_tensor_tensor(
                    pf2[:, g * 256:(g + 1) * 256], pf1[:, g * 259 + t: g * 259 + t + 256],
                    convw[:, 4 * g + t:4 * g + t + 1], pf2[:, g * 256:(g + 1) * 256],
                    ALU.mult, ALU.add))
        for g in range(18):
            em.op("vector", lambda e, g=g: e.tensor_copy(
                halo[:, g * 3:(g + 1) * 3], pf1[:, g * 259 + 256: g * 259 + 259]))
        fv = fence("vector")
        em.wait("scalar", S["vector"], fv)
        em.op("scalar", lambda e: e.activation(pf2[:, 0:4608], pf2[:, 0:4608], AF.Silu))
        fs_conv = fence("scalar")
        do_taps("conv", tile, {"xbcs": pf2[:, 0:4608], "dtT": dtT[:]})

        for so in range(4):
            p, wsem, wt = wload([("h", wz_hi_d, D_ZX, so * 512, 512, 8, 0),
                           ("l", wz_lo_d, D_ZX, so * 512, 512, 8, 0)])
            em.wait("tensor", wsem, wt)
            for c in range(2):
                bank = pb[4 + (so % 2) * 2 + c]
                for k in range(8):
                    ah = hh_[:, k * 256 + c * 128: k * 256 + (c + 1) * 128]
                    al = hl_[:, k * 256 + c * 128: k * 256 + (c + 1) * 128]
                    wh = wap(p, "h", k * 512, [[1, 512]])
                    wl = wap(p, "l", k * 512, [[1, 512]])
                    em.op("tensor", lambda e, b=bank, a=ah, w=wh, k=k: e.matmul(
                        b[:], a, w, start=(k == 0), stop=False))
                    em.op("tensor", lambda e, b=bank, a=al, w=wh: e.matmul(
                        b[:], a, w, start=False, stop=False))
                    em.op("tensor", lambda e, b=bank, a=ah, w=wl, k=k: e.matmul(
                        b[:], a, w, start=False, stop=(k == 7)))
            ftt = fence("tensor")
            wfree(p, ftt)
            em.wait("scalar", S["tensor"], ftt)
            for c in range(2):
                em.op("scalar", lambda e, so=so, c=c: e.copy(
                    zt[:, c * 2048 + so * 512: c * 2048 + (so + 1) * 512],
                    pb[4 + (so % 2) * 2 + c][:]))
            fs_z = fence("scalar")
            if so >= 1:
                em.wait("tensor", S["scalar"], fs_z)
        do_taps("zx", tile, {"zt": zt[:], "xbc_raw": pf1[:, 0:4608], "dt_f": dt_f[:]})

        em.wait("tensor", S["scalar"], fs_conv)
        em.wait("tensor", S["vector"], fv_dt)
        em.op("tensor", lambda e: e.matmul(
            pb[4][:, 0:64], triu[:], adt[:, 0:64], start=True, stop=False))
        em.op("tensor", lambda e: e.matmul(
            pb[4][:, 64:128], bass.AP(ones_c, 0, [[1, 128], [0, 128]]),
            adt[:, 0:64], start=False, stop=False))
        em.op("tensor", lambda e: e.matmul(
            pb[4][:, 128:192], bass.AP(ones_c, 0, [[1, 128], [0, 128]]),
            adt[:, 0:64], start=False, stop=False))
        em.op("tensor", lambda e: e.matmul(
            pb[4][:, 128:192], triu[:], adtn[:, 0:64], start=False, stop=True))
        ftt = fence("tensor")
        em.wait("scalar", S["tensor"], ftt)
        em.op("scalar", lambda e: e.activation(ddt[:], pb[4][:, 128:192], AF.Exp))
        em.op("scalar", lambda e: e.activation(eacst[:], pb[4][:, 0:64], AF.Exp))
        em.op("scalar", lambda e: e.activation(eab[:], pb[4][:, 64:128], AF.Exp))
        fs = fence("scalar")
        em.wait("vector", S["scalar"], fs)
        em.op("vector", lambda e: e.tensor_tensor(dtd[:], dtT[:], ddt[:], ALU.mult))
        fv_pre = fence("vector")
        em.wait("tensor", S["vector"], fv_pre)
        for c in range(2):
            bq = 16 * 256 + c * 128
            cq = 17 * 256 + c * 128
            em.op("tensor", lambda e, bq=bq, cq=cq: e.matmul(
                pb[5][:, 0:128], pf2[:, bq:bq + 128], pf2[:, cq:cq + 128],
                start=True, stop=True))
            em.op("tensor", lambda e, bq=bq: e.transpose(
                pb[5][:, 128:256], pf2[:, bq:bq + 128], ident[:]))
            ft_g = fence("tensor")
            for kk in range(4):
                for q in range(4):
                    k = kk * 4 + q
                    em.op("tensor", lambda e, k=k, kk=kk, q=q, c=c: e.transpose(
                        pb[kk][:, q * 128:(q + 1) * 128],
                        pf2[:, k * 256 + c * 128: k * 256 + c * 128 + 128],
                        ident[:]))
            ftt = fence("tensor")
            em.wait("vector", S["tensor"], ft_g)
            em.op("vector", lambda e: e.scalar_tensor_tensor(
                gtm[:], pb[5][:, 0:128], 1.0, triu[:], ALU.mult, ALU.mult))
            em.op("vector", lambda e: e.tensor_copy(btsb[:], pb[5][:, 128:256]))
            em.wait("vector", S["tensor"], ftt)
            for kk in range(4):
                em.op("vector", lambda e, kk=kk, c=c: e.tensor_tensor(
                    xdt[:, kk * 512:(kk + 1) * 512], pb[kk][:],
                    bass.AP(dtT, c * 32 + kk * 8, [[64, 128], [1, 8], [0, 64]]),
                    ALU.mult))
            for kk in range(4):
                em.op("vector", lambda e, kk=kk, c=c: e.tensor_tensor(
                    sxdt[:, kk * 512:(kk + 1) * 512], pb[kk][:],
                    bass.AP(dtd, c * 32 + kk * 8, [[64, 128], [1, 8], [0, 64]]),
                    ALU.mult))
            for kk in range(4):
                em.op("vector", lambda e, kk=kk: e.tensor_tensor(
                    pf7[:, kk * 512:(kk + 1) * 512], pb[kk][:],
                    bass.AP(dbc, kk * 8, [[32, 128], [1, 8], [0, 64]]), ALU.mult))
            fv_x = fence("vector")
            em.wait("tensor", S["vector"], fv_x)

            esb_s = [esb, esb2]
            mtsb_s = [mtsb, mtsb2]
            ft_e = {}; fv_min = {}; fs_exp = {}; fv_mt = {}; ft_y = {}; fv_ev = {}

            def emit_E(g):
                eb = pb[6 + (g % 2)]
                if g - 2 >= 0:
                    em.wait("tensor", S["scalar"], fv_min[g - 2])
                for i in range(4):
                    hd = g * 4 + i
                    lhs_a = bass.AP(adt, c * 32 + hd, [[64, 128], [0, 128]])
                    rhs_b = bass.AP(adtn, c * 32 + hd, [[64, 128], [0, 128]])
                    em.op("tensor", lambda e, eb=eb, i=i, a=lhs_a: e.matmul(
                        eb[:, i * 128:(i + 1) * 128], a, triu[:],
                        start=True, stop=False))
                    em.op("tensor", lambda e, eb=eb, i=i, rb=rhs_b: e.matmul(
                        eb[:, i * 128:(i + 1) * 128], triu[:], rb,
                        start=False, stop=True))
                ft_e[g] = fence("tensor")

            def emit_min(g):
                em.wait("scalar", S["tensor"], ft_e[g])
                if g - 2 >= 0:
                    em.wait("scalar", S["gpsimd"], fv_mt[g - 2])
                em.op("scalar", lambda e, g=g: e.activation(
                    esb_s[g % 2][:], pb[6 + (g % 2)][:], AF.Relu, scale=-1.0))
                fv_min[g] = fence("scalar")

            def emit_exp(g):
                em.wait("scalar", S["scalar"], fv_min[g])
                em.op("scalar", lambda e, g=g: e.activation(
                    esb_s[g % 2][:], esb_s[g % 2][:], AF.Exp, scale=-1.0))
                fs_exp[g] = fence("scalar")

            def emit_mt(g):
                em.wait("gpsimd", S["scalar"], fs_exp[g])
                if g - 2 >= 0:
                    em.wait("gpsimd", S["tensor"], ft_y[g - 2])
                em.op("gpsimd", lambda e, g=g: e.tensor_tensor(
                    mtsb_s[g % 2][:], esb_s[g % 2][:],
                    bass.AP(gtm, 0, [[128, 128], [0, 4], [1, 128]]), ALU.mult))
                fv_mt[g] = fence("gpsimd")

            def emit_Y(g):
                yb = pb[4 + (g % 2)]
                em.wait("tensor", S["gpsimd"], fv_mt[g])
                if g - 2 >= 0:
                    em.wait("tensor", S["vector"], fv_ev[g - 2])
                for i in range(4):
                    hd = g * 4 + i
                    em.op("tensor", lambda e, yb=yb, i=i, h=hd, g=g: e.matmul(
                        yb[:, i * 64:(i + 1) * 64], mtsb_s[g % 2][:, i * 128:(i + 1) * 128],
                        xdt[:, h * 64:(h + 1) * 64], start=True, stop=True))
                    em.op("tensor", lambda e, yb=yb, i=i, h=hd, cq=cq: e.matmul(
                        yb[:, 256 + i * 64: 256 + (i + 1) * 64], pf2[:, cq:cq + 128],
                        r_st[:, h * 64:(h + 1) * 64], start=True, stop=True))
                ft_y[g] = fence("tensor")

            fsc_ev = {}

            def emit_evac_s(g):
                yb = pb[4 + (g % 2)]
                em.wait("scalar", S["tensor"], ft_y[g])
                yo = (g % 2) * 256
                for i in range(4):
                    hd = g * 4 + i
                    em.op("scalar", lambda e, yb=yb, i=i, h=hd, c=c, yo=yo: e.activation(
                        yevt[:, yo + i * 64: yo + (i + 1) * 64],
                        yb[:, 256 + i * 64: 256 + (i + 1) * 64], AF.Identity,
                        scale=eacst[:, c * 32 + h: c * 32 + h + 1]))
                fsc_ev[g] = fence("scalar")

            def emit_evac(g):
                yb = pb[4 + (g % 2)]
                yo = (g % 2) * 256
                em.wait("vector", S["scalar"], fsc_ev[g])
                for i in range(4):
                    hd = g * 4 + i
                    em.op("vector", lambda e, yb=yb, i=i, h=hd, c=c, yo=yo: e.tensor_tensor(
                        pf3[:, c * 2048 + h * 64: c * 2048 + (h + 1) * 64],
                        yevt[:, yo + i * 64: yo + (i + 1) * 64],
                        yb[:, i * 64:(i + 1) * 64], ALU.add))
                fv_ev[g] = fence("vector")

            for g in range(10):
                if g < 8:
                    emit_E(g)
                if g == 6:
                    for sl in range(4):
                        em.op("tensor", lambda e, sl=sl: e.matmul(
                            pb[sl][:], btsb[:], sxdt[:, sl * 512:(sl + 1) * 512],
                            start=True, stop=True))
                    ft_s = fence("tensor")
                if g < 8:
                    emit_min(g)
                if g < 8:
                    emit_exp(g)
                if 0 <= g - 1 < 8:
                    emit_mt(g - 1)
                if 0 <= g - 1 < 8:
                    emit_Y(g - 1)
                if 0 <= g - 1 < 8:
                    emit_evac_s(g - 1)
                if 0 <= g - 2 < 8:
                    emit_evac(g - 2)
            _sf = fence("vector")
            em.wait("vector", S["vector"], _sf)
            em.op("vector", lambda e, c=c: e.tensor_tensor(
                pf3[:, c * 2048:(c + 1) * 2048], pf3[:, c * 2048:(c + 1) * 2048],
                pf7[:], ALU.add))
            fv_yd = fence("vector")
            em.wait("vector", S["tensor"], ft_s)
            for sl in range(4):
                em.op("vector", lambda e, sl=sl, c=c: e.tensor_tensor(
                    xts[:, sl * 512:(sl + 1) * 512], r_st[:, sl * 512:(sl + 1) * 512],
                    bass.AP(eab, c * 32 + sl * 8, [[64, 128], [1, 8], [0, 64]]),
                    ALU.mult))
            _sf = fence("vector")
            em.wait("vector", S["vector"], _sf)
            for sl in range(4):
                em.op("vector", lambda e, sl=sl: e.tensor_tensor(
                    r_st[:, sl * 512:(sl + 1) * 512], xts[:, sl * 512:(sl + 1) * 512],
                    pb[sl][:], ALU.add))
            fv_s = fence("vector")
            em.wait("tensor", S["vector"], fv_s)
        do_taps("ssd", tile, {"Y": pf3[:], "r": r_st[:]})

        em.wait("scalar", S["vector"], em.cur(S["vector"]))
        em.op("scalar", lambda e: e.activation(zt[:], zt[:], AF.Silu))
        fs = fence("scalar")
        em.wait("vector", S["scalar"], fs)
        for c in range(2):
            em.op("vector", lambda e, c=c: e.tensor_tensor(
                pf3[:, c * 2048:(c + 1) * 2048], pf3[:, c * 2048:(c + 1) * 2048],
                zt[:, c * 2048:(c + 1) * 2048], ALU.mult))
        _sf = fence("vector")
        em.wait("vector", S["vector"], _sf)
        for c in range(2):
            em.op("vector", lambda e, c=c: e.tensor_tensor(
                pf7[:, c * 1024:(c + 1) * 1024], pf3[:, c * 2048: c * 2048 + 1024],
                pf3[:, c * 2048: c * 2048 + 1024], ALU.mult))
            em.op("vector", lambda e, c=c: e.tensor_tensor(
                xdt[:, c * 1024:(c + 1) * 1024],
                pf3[:, c * 2048 + 1024: c * 2048 + 2048],
                pf3[:, c * 2048 + 1024: c * 2048 + 2048], ALU.mult))
        _sf = fence("vector")
        em.wait("vector", S["vector"], _sf)
        for c in range(2):
            em.op("vector", lambda e, c=c: e.tensor_reduce(
                bass.AP(stats, c, [[16, 128], [1, 1]]),
                bass.AP(pf7, c * 1024, [[2048, 128], [1, 1024]]), AXX, ALU.add))
        for c in range(2):
            em.op("vector", lambda e, c=c: e.tensor_reduce(
                bass.AP(stats, 12 + c, [[16, 128], [1, 1]]),
                bass.AP(xdt, c * 1024, [[2048, 128], [1, 1024]]), AXX, ALU.add))
        _sf = fence("vector")
        em.wait("vector", S["vector"], _sf)
        for c in range(2):
            em.op("vector", lambda e, c=c: e.tensor_tensor(
                stats[:, c:c + 1], stats[:, c:c + 1],
                stats[:, 12 + c:13 + c], ALU.add))
        fv = fence("vector")
        em.wait("scalar", S["vector"], fv)
        em.op("scalar", lambda e: e.activation(stats[:, 4:6], stats[:, 0:2], AF.Ln,
                                               bias=epsc[:, 0:1], scale=1.0 / 2048.0))
        _sfs = fence("scalar")
        em.wait("scalar", S["scalar"], _sfs)
        em.op("scalar", lambda e: e.activation(stats[:, 6:8], stats[:, 4:6], AF.Exp,
                                               scale=-0.5))
        fs = fence("scalar")
        em.wait("vector", S["scalar"], fs)
        for half in range(2):
            for c in range(2):
                em.op("vector", lambda e, c=c, hf=half: e.tensor_scalar(
                    pf7[:, c * 1024:(c + 1) * 1024],
                    pf3[:, c * 2048 + hf * 1024: c * 2048 + (hf + 1) * 1024],
                    stats[:, 6 + c:7 + c], None, ALU.mult))
            fv = fence("vector")
            em.wait("tensor", S["vector"], fv)

            def evac_g(k, kk, phase, half=half):
                kg = half * 8 + k
                if phase == 0:
                    em.op("scalar", lambda e, kg=kg, kk=kk: e.activation(
                        ghl[:, kg * 256:(kg + 1) * 256], pb[kk][:, 0:256],
                        AF.Identity, scale=ngw[:, kg:kg + 1]))
                else:
                    em.op("vector", lambda e, kg=kg, kk=kk: e.scalar_tensor_tensor(
                        ghl[:, 4096 + kg * 256: 4096 + (kg + 1) * 256], pb[kk][:, 0:256],
                        ngw[:, kg:kg + 1], ghl[:, kg * 256:(kg + 1) * 256],
                        ALU.mult, ALU.subtract))
            transpose_1024_block(pf7, evac_g)
        do_taps("gated", tile, {"ghl": None})

        em.wait("gpsimd", S["vector"], em.cur(S["vector"]))
        tx2 = None
        for c in range(2):
            tx2 = dma("gpsimd", DX, pf3[:, c * 1024:(c + 1) * 1024],
                      bass.AP(x_in, (trow + c * 128) * D, [[D, 128], [1, D]]))
        em.wait("vector", DX, tx2)
        em.op("vector", lambda e: e.tensor_tensor(
            outt[:], pf3[:, 0:2048], qt[:], ALU.add))
        fv = fence("vector")
        if tile + 1 < n_tiles:
            fv_ln = t0_load_ln(tile + 1)
        for so in range(2):
            for ks in range(2):
                p, wsem, wt = wload([("h", wo_hi_d, D, so * 512, 512, 8, ks * 8),
                               ("l", wo_lo_d, D, so * 512, 512, 8, ks * 8)])
                em.wait("tensor", wsem, wt)
                if so == 0 and ks == 0:
                    em.wait("tensor", S["vector"], fv)
                for c in range(2):
                    bank = pb[4 + 2 * so + c]
                    for kk in range(8):
                        kg = ks * 8 + kk
                        ah = ghl[:, kg * 256 + c * 128: kg * 256 + (c + 1) * 128]
                        al = bass.AP(ghl, 4096 + kg * 256 + c * 128,
                                     [[8192, 128], [1, 128]])
                        wh = wap(p, "h", kk * 512, [[1, 512]])
                        wl = wap(p, "l", kk * 512, [[1, 512]])
                        st = (ks == 0 and kk == 0)
                        sp = (ks == 1 and kk == 7)
                        em.op("tensor", lambda e, b=bank, a=ah, w=wh, st=st: e.matmul(
                            b[:], a, w, start=st, stop=False))
                        em.op("tensor", lambda e, b=bank, a=al, w=wh: e.matmul(
                            b[:], a, w, start=False, stop=False))
                        em.op("tensor", lambda e, b=bank, a=ah, w=wl, sp=sp: e.matmul(
                            b[:], a, w, start=False, stop=sp))
                ftt = fence("tensor")
                wfree(p, ftt)
        em.wait("vector", S["tensor"], ftt)
        for so in range(2):
            for c in range(2):
                em.op("vector", lambda e, so=so, c=c: e.tensor_tensor(
                    outt[:, c * 1024 + so * 512: c * 1024 + (so + 1) * 512],
                    outt[:, c * 1024 + so * 512: c * 1024 + (so + 1) * 512],
                    pb[4 + 2 * so + c][:], ALU.add))
        fv_wo = fence("vector")
        do_taps("wout", tile, {"outt": outt[:]})

        em.wait("tensor", S["vector"], fv_wo)
        if tile == 0:
            em.wait("sync", DGI[0], int_tok[0])
            em.wait("sync", DGI[1], int_tok[1])
        fv_prod = 0
        for s in range(6):
            p, wsem, wt = wload([("h", wg_b_d, HIDDEN, s * 512, 512, 8, 0),
                           ("l", wu_b_d, HIDDEN, s * 512, 512, 8, 0)])
            em.wait("tensor", wsem, wt)
            if s >= 1:
                em.wait("tensor", S["vector"], fv_prod)
            for m in range(4):
                bank = pb[m]
                for k in range(8):
                    wgt = wap(p, "h", k * 512 + m * 128, [[1, 128]])
                    em.op("tensor", lambda e, b=bank, w=wgt, k=k: e.matmul(
                        b[:, 0:256], w, xnh[:, k * 256:(k + 1) * 256],
                        start=(k == 0), stop=False))
                    em.op("tensor", lambda e, b=bank, w=wgt, k=k: e.matmul(
                        b[:, 0:256], w, xnl[:, k * 256:(k + 1) * 256],
                        start=False, stop=(k == 7)))
            ft_gate = fence("tensor")
            em.wait("vector", S["tensor"], ft_gate)
            for m in range(4):
                em.op("vector", lambda e, m=m, b=pb[m]: e.tensor_scalar(
                    pf1[:, m * 256:(m + 1) * 256], b[:, 0:256],
                    C127_LO, None, ALU.mult))
            for m in range(4):
                em.op("vector", lambda e, m=m, b=pb[m]: e.tensor_copy(
                    pf1[:, 2048 + m * 256: 2048 + (m + 1) * 256], b[:, 0:256]))
            fv_gate = fence("vector")
            em.wait("vector", S["vector"], fv_gate)
            for m in range(4):
                em.op("vector", lambda e, m=m: e.scalar_tensor_tensor(
                    pf1[:, 1024 + m * 256: 1024 + (m + 1) * 256],
                    pf1[:, 2048 + m * 256: 2048 + (m + 1) * 256], C127_HI,
                    pf1[:, m * 256:(m + 1) * 256], ALU.mult, ALU.add))
            fv = fence("vector")
            em.wait("scalar", S["vector"], fv)
            em.op("scalar", lambda e: e.activation(
                pf1[:, 0:1024], pf1[:, 1024:2048], AF.Silu))
            fs = fence("scalar")
            em.wait("tensor", S["vector"], fv_gate)
            for m in range(4):
                bank = pb[m]
                for k in range(8):
                    wut = wap(p, "l", k * 512 + m * 128, [[1, 128]])
                    em.op("tensor", lambda e, b=bank, w=wut, k=k: e.matmul(
                        b[:, 0:256], w, xnh[:, k * 256:(k + 1) * 256],
                        start=(k == 0), stop=False))
                    em.op("tensor", lambda e, b=bank, w=wut, k=k: e.matmul(
                        b[:, 0:256], w, xnl[:, k * 256:(k + 1) * 256],
                        start=False, stop=(k == 7)))
            ft_up = fence("tensor")
            em.wait("vector", S["tensor"], ft_up)
            em.wait("vector", S["scalar"], fs)
            for m in range(4):
                em.op("vector", lambda e, m=m, b=pb[m]: e.tensor_tensor(
                    pf2[:, m * 256:(m + 1) * 256], b[:, 0:256],
                    pf1[:, m * 256:(m + 1) * 256], ALU.mult))
            fv_prod = fence("vector")
            em.wait("scalar", S["vector"], fv_prod)
            for m in range(4):
                em.op("scalar", lambda e, m=m: e.copy(
                    ghl[:, m * 256:(m + 1) * 256], pf2[:, m * 256:(m + 1) * 256]))
            fs2 = fence("scalar")
            em.wait("vector", S["scalar"], fs2)
            for m in range(4):
                em.op("vector", lambda e, m=m: e.scalar_tensor_tensor(
                    ghl[:, 1024 + m * 256: 1024 + (m + 1) * 256],
                    pf2[:, m * 256:(m + 1) * 256], 1.0,
                    ghl[:, m * 256:(m + 1) * 256], ALU.mult, ALU.subtract))
            fv_hf = fence("vector")
            wfree(p, em.cur(S["tensor"]))
            p2, wsem2, wt = wload([("h", wd_b_d, D, 0, 1024, 4, s * 4)])
            em.wait("tensor", wsem2, wt)
            em.wait("tensor", S["vector"], fv_hf)
            for c in range(2):
                for os_ in range(2):
                    bank = pb[4 + 2 * os_ + c]
                    for k in range(4):
                        wdt = wap(p2, "h", k * 1024 + os_ * 512, [[1, 512]])
                        ah = ghl[:, k * 256 + c * 128: k * 256 + (c + 1) * 128]
                        al = ghl[:, 1024 + k * 256 + c * 128:
                                 1024 + k * 256 + (c + 1) * 128]
                        st = (s == 0 and k == 0)
                        sp = (s == 5 and k == 3)
                        em.op("tensor", lambda e, b=bank, a=ah, w=wdt, st=st: e.matmul(
                            b[:], a, w, start=st, stop=False))
                        em.op("tensor", lambda e, b=bank, a=al, w=wdt, sp=sp: e.matmul(
                            b[:], a, w, start=False, stop=sp))
            ftt = fence("tensor")
            wfree(p2, ftt)
        em.wait("vector", S["tensor"], ftt)
        for j, (c, os_) in enumerate([(c, o) for c in range(2) for o in range(2)]):
            bank = pb[4 + 2 * os_ + c]
            em.op("vector", lambda e, b=bank, j=j: e.tensor_scalar(
                pf2[:, j * 512:(j + 1) * 512], b[:], C2_LO, None, ALU.mult))
        _sf = fence("vector")
        em.wait("vector", S["vector"], _sf)
        for j, (c, os_) in enumerate([(c, o) for c in range(2) for o in range(2)]):
            bank = pb[4 + 2 * os_ + c]
            em.op("vector", lambda e, b=bank, j=j: e.scalar_tensor_tensor(
                pf2[:, 2048 + j * 512: 2048 + (j + 1) * 512], b[:], C2_HI,
                pf2[:, j * 512:(j + 1) * 512], ALU.mult, ALU.add))
        _sf = fence("vector")
        em.wait("vector", S["vector"], _sf)
        for j, (c, os_) in enumerate([(c, o) for c in range(2) for o in range(2)]):
            em.op("vector", lambda e, c=c, os_=os_, j=j: e.tensor_tensor(
                outt[:, c * 1024 + os_ * 512: c * 1024 + (os_ + 1) * 512],
                outt[:, c * 1024 + os_ * 512: c * 1024 + (os_ + 1) * 512],
                pf2[:, 2048 + j * 512: 2048 + (j + 1) * 512], ALU.add))
        fv_out = fence("vector")
        do_taps("ffn", tile, {"outt": outt[:]})

        em.wait("gpsimd", S["vector"], fv_out)
        for c in range(2):
            dma("gpsimd", DX, bass.AP(out_d, (trow + c * 128) * D, [[D, 128], [1, D]]),
                outt[:, c * 1024:(c + 1) * 1024])

    em.wait("gpsimd", DG, em.cur(DG))
    em.wait("gpsimd", DX, em.cur(DX))

    import contextlib
    with contextlib.ExitStack() as ctx:
        sems = {name: ctx.enter_context(nc.semaphore(name)) for name in em.sem_names}
        block = ctx.enter_context(nc.Block())

        def run_stream(stream):
            def go(eng_obj):
                for entry in stream:
                    if "wait" in entry:
                        s, v = entry["wait"]
                        eng_obj.wait_ge(sems[s], v)
                    else:
                        inst = entry["fn"](eng_obj)
                        for (s, n) in entry["incs"]:
                            inst.then_inc(sems[s], n)
            return go

        block.sync(run_stream(em.streams["sync"]))
        block.gpsimd(run_stream(em.streams["gpsimd"]))
        block.vector(run_stream(em.streams["vector"]))
        block.scalar(run_stream(em.streams["scalar"]))
        block.tensor(run_stream(em.streams["tensor"]))
    return nc


def stage_inputs(inputs):
    cos, sin = _rope_tables()
    f32c = np.ascontiguousarray
    shared = {
        "wq_t": f32c(np.asarray(inputs["L0_attn_W_in"])[:1024].T),
        "wz_t": f32c(np.asarray(inputs["L0_attn_W_zxbcdt"]).T),
        "wo_t": f32c(np.asarray(inputs["L0_attn_W_out"]).T),
        "wg_t": f32c(np.asarray(inputs["L0_ffn_w_gate"]).T),
        "wu_t": f32c(np.asarray(inputs["L0_ffn_w_up"]).T),
        "wd_t": f32c(np.asarray(inputs["L0_ffn_w_down"]).T),
        "conv_w": f32c(np.asarray(inputs["L0_attn_conv_w"]).reshape(18, 128, 4).transpose(1, 0, 2).reshape(128, 72)),
        "conv_b": f32c(np.asarray(inputs["L0_attn_conv_b"]).reshape(18, 128).T),
        "dt_bias": f32c(np.asarray(inputs["L0_attn_dt_bias"]).reshape(H_SSM, 1)),
        "a_log": f32c(np.asarray(inputs["L0_attn_A_log"]).reshape(1, H_SSM)),
        "d_par": f32c(np.asarray(inputs["L0_attn_D_param"]).reshape(1, H_SSM)),
        "npw": f32c(np.asarray(inputs["L0_attn_norm_pre_w"]).reshape(8, 128).T),
        "ngw": f32c(np.asarray(inputs["L0_attn_norm_gated_w"]).reshape(16, 128).T),
        "triu": f32c(np.triu(np.ones((128, 128), np.float32))),
        "ident": f32c(np.eye(128, dtype=np.float32)),
        "ones_c": np.ones((128, 1), np.float32),
        "ones_r": np.ones((1, 128), np.float32),
        "cos_t": cos,
        "sin_t": sin,
    }
    x = np.asarray(inputs["L0_attn_x"])
    return [dict(shared, x=np.ascontiguousarray(x[b])) for b in range(8)]


def kernel(**inputs):
    in_maps = stage_inputs(inputs)
    nc = build_program()
    res = run_bass_kernel_spmd(nc, in_maps, core_ids=list(range(8)))
    return np.stack([r["out"] for r in res.results]).astype(np.float32)


# version 23
# speedup vs baseline: 1.0255x; 1.0255x over previous
import numpy as np
import concourse.bass as bass
import concourse.mybir as mybir
from concourse.bass_utils import run_bass_kernel_spmd

F32 = mybir.dt.float32
BF16 = mybir.dt.bfloat16
I32 = mybir.dt.int32
AF = mybir.ActivationFunctionType
ALU = mybir.AluOpType
AXX = mybir.AxisListType.X

T_SEQ = 4096
T_TILE = 256
D = 1024
D_INNER = 2048
CONV_DIM = 2304
D_ZX = 4384
H_SSM = 32
HIDDEN = 3072

EPS = float(np.float32(1e-5))
C127_HI = float(np.float32(1.0 / 127.0))
C127_LO = float(np.float32(1.0 / 127.0 - np.float64(np.float32(1.0 / 127.0))))
C2 = 1.0 / (127.0 * 127.0)
C2_HI = float(np.float32(C2))
C2_LO = float(np.float32(C2 - np.float64(np.float32(C2))))

ENGINES = ("sync", "gpsimd", "vector", "scalar", "tensor")


def _rope_tables():
    import jax
    import jax.numpy as jnp
    cpu = jax.devices("cpu")[0]
    with jax.default_device(cpu):
        inv_freq = 1.0 / 10000.0 ** (jnp.arange(0, 64, 2, dtype=jnp.float32) / 64)
        freqs = jnp.arange(T_SEQ, dtype=jnp.float32)[:, None] * inv_freq[None, :]
        cos = np.asarray(jnp.cos(freqs), dtype=np.float32)
        sin = np.asarray(jnp.sin(freqs), dtype=np.float32)
    return np.ascontiguousarray(cos), np.ascontiguousarray(sin)


class Em:

    def __init__(self):
        self.streams = {e: [] for e in ENGINES}
        self.planned = {}
        self.sem_names = []

    def sem(self, name):
        if name not in self.planned:
            self.planned[name] = 0
            self.sem_names.append(name)
        return name

    def op(self, eng, fn):
        self.streams[eng].append({"fn": fn, "incs": []})

    def inc(self, eng, sem, n=1):
        self.sem(sem)
        assert self.streams[eng] and "fn" in self.streams[eng][-1]
        for (s0, _) in self.streams[eng][-1]["incs"]:
            if s0 == sem:
                return self.planned[sem]
        self.streams[eng][-1]["incs"].append((sem, n))
        self.planned[sem] += n
        return self.planned[sem]

    def wait(self, eng, sem, val):
        self.sem(sem)
        if val > 0:
            self.streams[eng].append({"wait": (sem, val)})

    def cur(self, sem):
        return self.planned.get(sem, 0)


def build_program(n_tiles=16, debug_taps=()):
    nc = bass.Bass(target_bir_lowering=False, trn_type="TRN2")

    x_in = nc.dram_tensor("x", [T_SEQ, D], F32, kind="ExternalInput")
    out_d = nc.dram_tensor("out", [T_SEQ, D], F32, kind="ExternalOutput")

    wq_td = nc.dram_tensor("wq_t", [D, D], F32, kind="ExternalInput")
    wz_td = nc.dram_tensor("wz_t", [D, D_ZX], F32, kind="ExternalInput")
    wo_td = nc.dram_tensor("wo_t", [D_INNER, D], F32, kind="ExternalInput")
    wg_td = nc.dram_tensor("wg_t", [D, HIDDEN], I32, kind="ExternalInput")
    wu_td = nc.dram_tensor("wu_t", [D, HIDDEN], I32, kind="ExternalInput")
    wd_td = nc.dram_tensor("wd_t", [HIDDEN, D], I32, kind="ExternalInput")
    convw_d = nc.dram_tensor("conv_w", [128, 72], F32, kind="ExternalInput")
    convb_d = nc.dram_tensor("conv_b", [128, 18], F32, kind="ExternalInput")
    dtb_d = nc.dram_tensor("dt_bias", [H_SSM, 1], F32, kind="ExternalInput")
    alog_d = nc.dram_tensor("a_log", [1, H_SSM], F32, kind="ExternalInput")
    dpar_d = nc.dram_tensor("d_par", [1, H_SSM], F32, kind="ExternalInput")
    npw_d = nc.dram_tensor("npw", [128, 8], F32, kind="ExternalInput")
    ngw_d = nc.dram_tensor("ngw", [128, 16], F32, kind="ExternalInput")
    triu_d = nc.dram_tensor("triu", [128, 128], F32, kind="ExternalInput")
    ident_d = nc.dram_tensor("ident", [128, 128], F32, kind="ExternalInput")
    onesc_d = nc.dram_tensor("ones_c", [128, 1], F32, kind="ExternalInput")
    ones_d = nc.dram_tensor("ones_r", [1, 128], F32, kind="ExternalInput")
    cos_d = nc.dram_tensor("cos_t", [T_SEQ, 32], F32, kind="ExternalInput")
    sin_d = nc.dram_tensor("sin_t", [T_SEQ, 32], F32, kind="ExternalInput")

    wg_b_d = nc.dram_tensor("wg_b", [D, HIDDEN], BF16, kind="Internal")
    wu_b_d = nc.dram_tensor("wu_b", [D, HIDDEN], BF16, kind="Internal")
    wd_b_d = nc.dram_tensor("wd_b", [HIDDEN, D], BF16, kind="Internal")
    wq_hi_d = nc.dram_tensor("wq_hi", [D, D], BF16, kind="Internal")
    wq_lo_d = nc.dram_tensor("wq_lo", [D, D], BF16, kind="Internal")
    wz_hi_d = nc.dram_tensor("wz_hi", [D, D_ZX], BF16, kind="Internal")
    wz_lo_d = nc.dram_tensor("wz_lo", [D, D_ZX], BF16, kind="Internal")
    wo_hi_d = nc.dram_tensor("wo_hi", [D_INNER, D], BF16, kind="Internal")
    wo_lo_d = nc.dram_tensor("wo_lo", [D_INNER, D], BF16, kind="Internal")

    dbg_ds = [nc.dram_tensor(f"dbg{i}", list(shape), F32, kind="ExternalOutput")
              for i, (_, _, shape) in enumerate(debug_taps)]

    A = nc.alloc_sbuf_tensor
    triu = A("c_triu", [128, 128], F32)
    ident = A("c_ident", [128, 128], F32)
    ones_c = A("c_onesc", [128, 1], F32)
    ones_r = A("c_onesr", [1, 128], F32)
    cosb = A("c_cos", [128, 64], F32)
    sinb = A("c_sin", [128, 64], F32)
    abc = A("c_abc", [128, 32], F32)
    abcn = A("c_abcn", [128, 32], F32)
    dbc = A("c_dbc", [128, 32], F32)
    arow = A("c_arow", [1, 32], F32)
    drow = A("c_drow", [1, 32], F32)
    dtbc = A("c_dtb", [32, 1], F32)
    convw = A("c_convw", [128, 72], F32)
    convb = A("c_convb", [128, 18], F32)
    npw = A("c_npw", [128, 8], F32)
    ngw = A("c_ngw", [128, 16], F32)
    halo = A("b_halo", [128, 54], F32)
    epsc = A("c_eps", [128, 1], F32)

    dt_f = A("b_dtf", [32, 256], F32)
    dt_e = A("b_dte", [32, 256], F32)
    dtT = A("b_dtT", [128, 64], F32)
    adt = A("b_adt", [128, 64], F32)
    adtn = A("b_adtn", [128, 64], F32)
    acst = A("b_acst", [128, 64], F32)
    tmp32 = A("b_tmp32", [128, 64], F32)
    ddt = A("b_ddt", [128, 64], F32)
    eacst = A("b_eacst", [128, 64], F32)
    eab = A("b_eab", [128, 64], F32)
    dtd = A("b_dtd", [128, 64], F32)
    stats = A("b_stats", [128, 16], F32)
    esb = A("b_esb", [128, 512], F32)
    mtsb = A("b_mtsb", [128, 512], F32)
    gtm = A("b_gtm", [128, 128], F32)
    btsb = A("b_btsb", [128, 128], F32)

    qt = A("b_qt", [128, 2048], F32)
    xnh = A("b_xnh", [128, 2048], BF16)
    xnl = A("b_xnl", [128, 2048], BF16)
    hh_ = A("b_hh", [128, 2048], BF16)
    hl_ = A("b_hl", [128, 2048], BF16)
    zt = A("b_zt", [128, 4096], F32)
    pf1 = A("b_pf1", [128, 4662], F32)
    pf2 = A("b_pf2", [128, 4608], F32)
    pf3 = A("b_pf3", [128, 4096], F32)
    pf7 = A("b_pf7", [128, 2048], F32)
    xts = A("b_xts", [128, 2048], F32)
    xdt = A("b_xdt", [128, 2048], F32)
    sxdt = A("b_sxdt", [128, 2048], F32)
    r_st = A("b_r", [128, 2048], F32)
    ghl = A("b_ghl", [128, 8192], BF16)
    esb2 = A("b_esb2", [128, 512], F32)
    mtsb2 = A("b_mtsb2", [128, 512], F32)
    yevt = A("b_yevt", [128, 512], F32)
    outt = A("b_out", [128, 2048], F32)
    wbh = A("b_wbh", [128, 4096], BF16)
    wbl = A("b_wbl", [128, 4096], BF16)
    wbh2 = A("b_wbh2", [128, 4096], BF16)
    wbl2 = A("b_wbl2", [128, 4096], BF16)

    pb = [nc.alloc_psum_tensor(f"pb{i}", [128, 512], F32) for i in range(8)]


    em = Em()
    S = {e: f"S_{e}" for e in ENGINES}
    DQ = "S_dma_sync"
    DG = "S_dma_gps"
    DC_ = "S_dma_const"
    DA = "S_dma_aux"
    DW = ["S_dma_w0", "S_dma_w1"]
    DX = "S_dma_x"

    def dma(eng_q, sem, out, in_):
        em.op(eng_q, lambda e, o=out, i=in_: e.dma_start(out=o, in_=i))
        return em.inc(eng_q, sem, 16)

    def fence(eng):
        return em.inc(eng, S[eng], 1)

    dma("sync", DC_, triu[:], triu_d.ap())
    dma("sync", DC_, ident[:], ident_d.ap())
    dma("sync", DC_, ones_c[:], onesc_d.ap())
    dma("sync", DC_, ones_r[:], ones_d.ap())
    dma("sync", DC_, arow[:], alog_d.ap())
    dma("sync", DC_, drow[:], dpar_d.ap())
    dma("sync", DC_, dtbc[:], dtb_d.ap())
    dma("sync", DC_, convw[:], convw_d.ap())
    dma("sync", DC_, convb[:], convb_d.ap())
    dma("sync", DC_, npw[:], npw_d.ap())
    ld0 = dma("sync", DC_, ngw[:], ngw_d.ap())

    em.wait("scalar", DC_, ld0)
    em.op("scalar", lambda e: e.activation(arow[:], arow[:], AF.Exp))
    fs = fence("scalar")
    em.wait("vector", S["scalar"], fs)
    em.op("vector", lambda e: e.tensor_scalar_mul(arow[:], arow[:], -1.0))
    em.op("vector", lambda e: e.memset(halo[:], 0.0))
    em.op("vector", lambda e: e.memset(epsc[:], EPS))
    em.op("vector", lambda e: e.memset(r_st[:], 0.0))
    fv = fence("vector")
    em.wait("tensor", S["vector"], fv)
    em.op("tensor", lambda e: e.matmul(pb[0][:, 0:32], ones_r[:], arow[:],
                                       start=True, stop=True))
    em.op("tensor", lambda e: e.matmul(pb[0][:, 32:64], ones_r[:], drow[:],
                                       start=True, stop=True))
    ft = fence("tensor")
    em.wait("vector", S["tensor"], ft)
    em.op("vector", lambda e: e.tensor_copy(abc[:], pb[0][:, 0:32]))
    em.op("vector", lambda e: e.tensor_copy(dbc[:], pb[0][:, 32:64]))
    em.op("vector", lambda e: e.tensor_scalar(abcn[:], pb[0][:, 0:32], -1.0, None, ALU.mult))
    fence("vector")

    dsem = ["S_dma_prepg0", "S_dma_prepg1"]

    def ghq(pi, w, lo=False):
        return bass.AP(ghl, pi * 2048 + (4096 if lo else 0),
                       [[8192, 128], [1, w]])

    def prep_fp32(src_d, hi_d, lo_d, K, Dout):
        pieces = [(i * 2048, min(2048, Dout - i * 2048))
                  for i in range((Dout + 2047) // 2048)]
        stage = [pf1, pf2]
        ssem = ["S_dma_prep0", "S_dma_prep1"]
        fv_done = {}
        pi_ctr = [0]
        st_tok = {0: 0, 1: 0}
        for k in range(K // 128):
            sb = stage[k % 2]
            if k >= 2:
                em.wait("sync", S["vector"], fv_done[k - 2])
            t = dma("sync", ssem[k % 2], sb[:, 0:Dout],
                    bass.AP(src_d, k * 128 * Dout, [[Dout, 128], [1, Dout]]))
            em.wait("scalar", ssem[k % 2], t)
            for (c0, w) in pieces:
                pi = pi_ctr[0] % 2
                pi_ctr[0] += 1
                em.wait("scalar", dsem[pi], st_tok[pi])
                em.op("scalar", lambda e, sb=sb, c0=c0, w=w, pi=pi: e.copy(
                    ghq(pi, w), sb[:, c0:c0 + w]))
                fs2 = fence("scalar")
                em.wait("vector", S["scalar"], fs2)
                em.wait("vector", dsem[pi], st_tok[pi])
                em.op("vector", lambda e, sb=sb, c0=c0, w=w, pi=pi: e.scalar_tensor_tensor(
                    ghq(pi, w, True), sb[:, c0:c0 + w], 1.0, ghq(pi, w),
                    ALU.mult, ALU.subtract))
                fv2 = fence("vector")
                em.wait("gpsimd", S["vector"], fv2)
                dma("gpsimd", dsem[pi],
                    bass.AP(hi_d, k * 128 * Dout + c0, [[Dout, 128], [1, w]]),
                    ghq(pi, w))
                st_tok[pi] = dma("gpsimd", dsem[pi],
                                 bass.AP(lo_d, k * 128 * Dout + c0,
                                         [[Dout, 128], [1, w]]),
                                 ghq(pi, w, True))
            fv_done[k] = fv2
        for pi in range(2):
            em.wait("sync", dsem[pi], st_tok[pi])
            em.wait("gpsimd", dsem[pi], st_tok[pi])
        return dict(st_tok)

    DGI = [DG, "S_dma_int1"]
    DGL = "S_dma_gl"
    int_last = {0: None, 1: None}
    int_tok = [0, 0]

    def gint(pi, w, second=False):
        return bass.AP(ghl, (4096 + pi * 1024) if second else pi * 2048,
                       [[8192, 128], [1, w]])

    def prep_int(src_d, dst_d, K, Dout):
        w1 = min(2048, Dout)
        w2 = Dout - w1
        for k in range(K // 128):
            pi = k % 2
            if int_last[pi] is not None:
                em.wait("gpsimd", DGI[pi], int_last[pi])
            t = dma("gpsimd", DGL, gint(pi, w1),
                    bass.AP(src_d, k * 128 * Dout, [[Dout, 128], [1, w1]]))
            if w2:
                t = dma("gpsimd", DGL, gint(pi, w2, True),
                        bass.AP(src_d, k * 128 * Dout + w1,
                                [[Dout, 128], [1, w2]]))
            em.wait("gpsimd", DGL, t)
            t2 = dma("gpsimd", DGI[pi],
                     bass.AP(dst_d, k * 128 * Dout, [[Dout, 128], [1, w1]]),
                     gint(pi, w1))
            if w2:
                t2 = dma("gpsimd", DGI[pi],
                         bass.AP(dst_d, k * 128 * Dout + w1,
                                 [[Dout, 128], [1, w2]]),
                         gint(pi, w2, True))
            int_last[pi] = t2
            int_tok[pi] = t2

    wpair = [{"ht": wbh, "hb": 0, "lt": wbl, "lb": 0, "stride": 4096,
              "free": 0, "sem": DW[0], "gsem": "S_dma_wg0"},
             {"ht": wbh2, "hb": 0, "lt": wbl2, "lb": 0, "stride": 4096,
              "free": 0, "sem": DW[1], "gsem": "S_dma_wg1"},
             {"ht": ghl, "hb": 0, "lt": ghl, "lb": 4096, "stride": 8192,
              "free": 0, "sem": "S_dma_w2", "gsem": "S_dma_wg2"}]
    widx = [0]

    def wap(p, which, off, rest):
        return bass.AP(p[which + "t"], p[which + "b"] + off,
                       [[p["stride"], 128]] + rest)

    def wload(specs, eng="sync", n=2):
        p = wpair[widx[0] % n]
        widx[0] += 1
        em.wait(eng, S["tensor"], p["free"])
        sem = p["sem"] if eng == "sync" else p["gsem"]
        tok = 0
        for (which, src_d, Dout, col0, width, kchunks, k0) in specs:
            em.op(eng, lambda e, p=p, which=which, src=src_d, Do=Dout,
                  c0=col0, w=width, kc=kchunks, kz=k0: e.dma_start(
                      out=wap(p, which, 0, [[w, kc], [1, w]]),
                      in_=bass.AP(src, (kz * 128) * Do + c0,
                                  [[Do, 128], [128 * Do, kc], [1, w]])))
            tok = em.inc(eng, sem, 16)
        return p, sem, tok

    def wfree(p, ftok):
        p["free"] = ftok

    trope_box = [0]

    def t0_load_ln(tile2):
        trow2 = tile2 * T_TILE
        if tile2 == 0:
            tx = tx0_box[0]
            xsem = "S_dma_x0"
        else:
            em.wait("gpsimd", S["vector"], em.cur(S["vector"]))
            tx = None
            xsem = DX
            for c in range(2):
                tx = dma("gpsimd", DX, pf3[:, c * 1024:(c + 1) * 1024],
                         bass.AP(x_in, (trow2 + c * 128) * D, [[D, 128], [1, D]]))
        dma("sync", DA, bass.AP(cosb, 0, [[64, 128], [32, 2], [1, 32]]),
            bass.AP(cos_d, trow2 * 32, [[32, 128], [32 * 128, 2], [1, 32]]))
        trope2 = dma("sync", DA, bass.AP(sinb, 0, [[64, 128], [32, 2], [1, 32]]),
                     bass.AP(sin_d, trow2 * 32, [[32, 128], [32 * 128, 2], [1, 32]]))
        trope_box[0] = trope2
        em.wait("vector", xsem, tx)
        em.op("vector", lambda e: e.tensor_reduce(
            bass.AP(stats, 0, [[16, 128], [1, 2]]),
            bass.AP(pf3, 0, [[4096, 128], [1024, 2], [1, 1024]]), AXX, ALU.add))
        fv = fence("vector")
        em.wait("scalar", S["vector"], fv)
        em.op("scalar", lambda e: e.activation(stats[:, 8:10], stats[:, 0:2],
                                               AF.Identity, scale=1.0 / 1024.0))
        fs = fence("scalar")
        em.wait("vector", S["scalar"], fs)
        for c in range(2):
            em.op("vector", lambda e, c=c: e.tensor_scalar(
                pf3[:, c * 1024:(c + 1) * 1024], pf3[:, c * 1024:(c + 1) * 1024],
                stats[:, 8 + c:9 + c], None, ALU.subtract))
        _sf = fence("vector")
        em.wait("vector", S["vector"], _sf)
        for c in range(2):
            em.op("vector", lambda e, c=c: e.tensor_tensor(
                pf7[:, c * 1024:(c + 1) * 1024], pf3[:, c * 1024:(c + 1) * 1024],
                pf3[:, c * 1024:(c + 1) * 1024], ALU.mult))
        _sf = fence("vector")
        em.wait("vector", S["vector"], _sf)
        for c in range(2):
            em.op("vector", lambda e, c=c: e.tensor_reduce(
                bass.AP(stats, 2 + c, [[16, 128], [1, 1]]),
                bass.AP(pf7, c * 1024, [[2048, 128], [1, 1024]]), AXX, ALU.add))
        fv = fence("vector")
        em.wait("scalar", S["vector"], fv)
        em.op("scalar", lambda e: e.activation(stats[:, 4:6], stats[:, 2:4], AF.Ln,
                                               bias=epsc[:, 0:1], scale=1.0 / 1024.0))
        _sfs = fence("scalar")
        em.wait("scalar", S["scalar"], _sfs)
        em.op("scalar", lambda e: e.activation(stats[:, 6:8], stats[:, 4:6], AF.Exp,
                                               scale=-0.5))
        fs = fence("scalar")
        em.wait("vector", S["scalar"], fs)
        for c in range(2):
            em.op("vector", lambda e, c=c: e.tensor_scalar(
                pf3[:, c * 1024:(c + 1) * 1024], pf3[:, c * 1024:(c + 1) * 1024],
                stats[:, 6 + c:7 + c], None, ALU.mult))
        return fence("vector")

    tx0_box = [None]
    for c in range(2):
        tx0_box[0] = dma("sync", "S_dma_x0", pf3[:, c * 1024:(c + 1) * 1024],
                         bass.AP(x_in, (c * 128) * D, [[D, 128], [1, D]]))
    prep_fp32(wq_td, wq_hi_d, wq_lo_d, D, D)
    prep_fp32(wz_td, wz_hi_d, wz_lo_d, D, D_ZX)
    prep_fp32(wo_td, wo_hi_d, wo_lo_d, D_INNER, D)
    em.wait("gpsimd", S["scalar"], em.cur(S["scalar"]))
    em.wait("gpsimd", S["vector"], em.cur(S["vector"]))
    prep_int(wg_td, wg_b_d, D, HIDDEN)
    prep_int(wu_td, wu_b_d, D, HIDDEN)
    prep_int(wd_td, wd_b_d, HIDDEN, D)
    em.wait("gpsimd", DGI[0], int_tok[0])
    em.wait("gpsimd", DGI[1], int_tok[1])

    tapmap = {}
    for i, (phase, buf, shape) in enumerate(debug_taps):
        tapmap.setdefault(phase, []).append((i, buf))

    def do_taps(phase, tile, locs):
        if tile != 0 or phase not in tapmap:
            return
        for i, buf in tapmap[phase]:
            em.wait("gpsimd", S["vector"], em.cur(S["vector"]))
            em.wait("gpsimd", S["scalar"], em.cur(S["scalar"]))
            em.wait("gpsimd", S["tensor"], em.cur(S["tensor"]))
            em.wait("gpsimd", DG, em.cur(DG))
            em.wait("gpsimd", DX, em.cur(DX))
            em.wait("gpsimd", DQ, em.cur(DQ))
            em.wait("gpsimd", DC_, em.cur(DC_))
            em.wait("gpsimd", DA, em.cur(DA))
            t = dma("gpsimd", DX, dbg_ds[i].ap(), locs[buf])
            em.wait("gpsimd", DX, t)
            em.wait("vector", DX, t)
            em.wait("scalar", DX, t)
            em.wait("tensor", DX, t)
            em.wait("sync", DX, t)

    def transpose_1024_block(src, evac):
        for rnd in range(2):
            for kk in range(4):
                k = rnd * 4 + kk
                for c in range(2):
                    em.op("tensor", lambda e, k=k, c=c, kk=kk: e.transpose(
                        pb[kk][:, c * 128:(c + 1) * 128],
                        src[:, c * 1024 + k * 128: c * 1024 + (k + 1) * 128],
                        ident[:]))
            ftt = fence("tensor")
            em.wait("scalar", S["tensor"], ftt)
            for kk in range(4):
                evac(rnd * 4 + kk, kk, 0)
            fss = fence("scalar")
            em.wait("vector", S["scalar"], fss)
            for kk in range(4):
                evac(rnd * 4 + kk, kk, 1)
            fvv = fence("vector")
            em.wait("tensor", S["vector"], fvv)
        return fvv

    fv_ln = 0
    for tile in range(n_tiles):
        trow = tile * T_TILE


        if tile == 0:
            fv_ln = t0_load_ln(0)
        fv = fv_ln
        trope = trope_box[0]
        do_taps("ln", tile, {"xn_t": pf3[:, 0:2048]})
        em.wait("tensor", S["vector"], fv)

        def evac_xn(k, kk, phase):
            if phase == 0:
                em.op("scalar", lambda e, k=k, kk=kk: e.copy(
                    xnh[:, k * 256:(k + 1) * 256], pb[kk][:, 0:256]))
            else:
                em.op("vector", lambda e, k=k, kk=kk: e.scalar_tensor_tensor(
                    xnl[:, k * 256:(k + 1) * 256], pb[kk][:, 0:256], 1.0,
                    xnh[:, k * 256:(k + 1) * 256], ALU.mult, ALU.subtract))
        transpose_1024_block(pf3, evac_xn)

        for so in range(2):
            p, wsem, wt = wload([("h", wq_hi_d, D, so * 512, 512, 8, 0),
                           ("l", wq_lo_d, D, so * 512, 512, 8, 0)])
            em.wait("tensor", wsem, wt)
            for c in range(2):
                bank = pb[4 + 2 * so + c]
                for k in range(8):
                    ah = xnh[:, k * 256 + c * 128: k * 256 + (c + 1) * 128]
                    al = xnl[:, k * 256 + c * 128: k * 256 + (c + 1) * 128]
                    wh = wap(p, "h", k * 512, [[1, 512]])
                    wl = wap(p, "l", k * 512, [[1, 512]])
                    em.op("tensor", lambda e, b=bank, a=ah, w=wh, k=k: e.matmul(
                        b[:], a, w, start=(k == 0), stop=False))
                    em.op("tensor", lambda e, b=bank, a=al, w=wh: e.matmul(
                        b[:], a, w, start=False, stop=False))
                    em.op("tensor", lambda e, b=bank, a=ah, w=wl, k=k: e.matmul(
                        b[:], a, w, start=False, stop=(k == 7)))
            ftt = fence("tensor")
            wfree(p, ftt)
        em.wait("scalar", S["tensor"], ftt)
        for so in range(2):
            for c in range(2):
                em.op("scalar", lambda e, so=so, c=c: e.copy(
                    pf7[:, c * 1024 + so * 512: c * 1024 + (so + 1) * 512],
                    pb[4 + 2 * so + c][:]))
        fs_q = fence("scalar")
        em.wait("vector", S["scalar"], fs_q)
        em.wait("vector", DA, trope)
        combines = []
        for c in range(2):
            base = c * 1024
            qr_lo = bass.AP(pf7, base, [[2048, 128], [64, 16], [1, 32]])
            qr_hi = bass.AP(pf7, base + 32, [[2048, 128], [64, 16], [1, 32]])
            cos_ap = bass.AP(cosb, c * 32, [[64, 128], [0, 16], [1, 32]])
            sin_ap = bass.AP(sinb, c * 32, [[64, 128], [0, 16], [1, 32]])
            qt_lo = bass.AP(qt, base, [[2048, 128], [64, 16], [1, 32]])
            qt_hi = bass.AP(qt, base + 32, [[2048, 128], [64, 16], [1, 32]])
            tbuf = xdt if c == 0 else sxdt
            t_ap = lambda j, tb=tbuf: bass.AP(tb, j * 512, [[2048, 128], [32, 16], [1, 32]])
            em.op("vector", lambda e, a=qr_lo, b=cos_ap, o=t_ap(0): e.tensor_tensor(o, a, b, ALU.mult))
            em.op("vector", lambda e, a=qr_hi, b=sin_ap, o=t_ap(1): e.tensor_tensor(o, a, b, ALU.mult))
            em.op("vector", lambda e, a=qr_hi, b=cos_ap, o=t_ap(2): e.tensor_tensor(o, a, b, ALU.mult))
            em.op("vector", lambda e, a=qr_lo, b=sin_ap, o=t_ap(3): e.tensor_tensor(o, a, b, ALU.mult))
            combines.append((qt_lo, t_ap(0), t_ap(1), ALU.subtract))
            combines.append((qt_hi, t_ap(2), t_ap(3), ALU.add))
        _sf = fence("vector")
        em.wait("vector", S["vector"], _sf)
        for (o, a, b, op_) in combines:
            em.op("vector", lambda e, o=o, a=a, b=b, op_=op_: e.tensor_tensor(o, a, b, op_))
        fv = fence("vector")
        do_taps("rope", tile, {"qt": qt[:]})

        _sf = fence("vector")
        em.wait("vector", S["vector"], _sf)
        for c in range(2):
            em.op("vector", lambda e, c=c: e.tensor_tensor(
                pf7[:, c * 1024:(c + 1) * 1024], qt[:, c * 1024:(c + 1) * 1024],
                qt[:, c * 1024:(c + 1) * 1024], ALU.mult))
        _sf = fence("vector")
        em.wait("vector", S["vector"], _sf)
        for c in range(2):
            em.op("vector", lambda e, c=c: e.tensor_reduce(
                bass.AP(stats, c, [[16, 128], [1, 1]]),
                bass.AP(pf7, c * 1024, [[2048, 128], [1, 1024]]), AXX, ALU.add))
        fv = fence("vector")
        em.wait("scalar", S["vector"], fv)
        em.op("scalar", lambda e: e.activation(stats[:, 4:6], stats[:, 0:2], AF.Ln,
                                               bias=epsc[:, 0:1], scale=1.0 / 1024.0))
        _sfs = fence("scalar")
        em.wait("scalar", S["scalar"], _sfs)
        em.op("scalar", lambda e: e.activation(stats[:, 6:8], stats[:, 4:6], AF.Exp,
                                               scale=-0.5))
        fs = fence("scalar")
        em.wait("vector", S["scalar"], fs)
        for c in range(2):
            em.op("vector", lambda e, c=c: e.tensor_scalar(
                pf7[:, c * 1024:(c + 1) * 1024], qt[:, c * 1024:(c + 1) * 1024],
                stats[:, 6 + c:7 + c], None, ALU.mult))
        fv = fence("vector")
        em.wait("tensor", S["vector"], fv)

        def evac_h(k, kk, phase):
            if phase == 0:
                em.op("scalar", lambda e, k=k, kk=kk: e.activation(
                    hh_[:, k * 256:(k + 1) * 256], pb[kk][:, 0:256],
                    AF.Identity, scale=npw[:, k:k + 1]))
            else:
                em.op("vector", lambda e, k=k, kk=kk: e.scalar_tensor_tensor(
                    hl_[:, k * 256:(k + 1) * 256], pb[kk][:, 0:256], npw[:, k:k + 1],
                    hh_[:, k * 256:(k + 1) * 256], ALU.mult, ALU.subtract))
        transpose_1024_block(pf7, evac_h)

        fv_evac = 0
        for sf in range(5):
            col0 = D_INNER + sf * 512
            width = min(512, D_ZX - col0)
            p, wsem, wt = wload([("h", wz_hi_d, D_ZX, col0, width, 8, 0),
                           ("l", wz_lo_d, D_ZX, col0, width, 8, 0)])
            em.wait("tensor", wsem, wt)
            if sf >= 1:
                em.wait("tensor", S["vector"], fv_evac)
            nm = (width + 127) // 128
            for m in range(nm):
                mw = min(128, width - m * 128)
                bank = pb[m % 4]
                for k in range(8):
                    wh = wap(p, "h", k * width + m * 128, [[1, mw]])
                    wl = wap(p, "l", k * width + m * 128, [[1, mw]])
                    rh = hh_[:, k * 256:(k + 1) * 256]
                    rl = hl_[:, k * 256:(k + 1) * 256]
                    em.op("tensor", lambda e, b=bank, w=wh, r=rh, k=k, mw=mw: e.matmul(
                        b[0:mw, 0:256], w, r, start=(k == 0), stop=False))
                    em.op("tensor", lambda e, b=bank, w=wl, r=rh, mw=mw: e.matmul(
                        b[0:mw, 0:256], w, r, start=False, stop=False))
                    em.op("tensor", lambda e, b=bank, w=wh, r=rl, k=k, mw=mw: e.matmul(
                        b[0:mw, 0:256], w, r, start=False, stop=(k == 7)))
            ftt = fence("tensor")
            wfree(p, ftt)
            em.wait("vector", S["tensor"], ftt)
            for m in range(nm):
                g = sf * 4 + m
                bank = pb[m % 4]
                if g < 18:
                    em.op("vector", lambda e, g=g, b=bank: e.tensor_copy(
                        pf1[:, g * 259 + 3: g * 259 + 259], b[:, 0:256]))
                else:
                    em.op("vector", lambda e, b=bank: e.tensor_copy(
                        dt_f[:, 0:256], b[0:32, 0:256]))
            fv_evac = fence("vector")
        em.wait("tensor", S["vector"], fv_evac)
        for g in range(18):
            em.op("vector", lambda e, g=g: e.tensor_copy(
                pf1[:, g * 259: g * 259 + 3], halo[:, g * 3:(g + 1) * 3]))
        fv = fence("vector")
        em.wait("scalar", S["vector"], fv)
        em.op("scalar", lambda e: e.activation(dt_e[:], dt_f[:], AF.Exp, bias=dtbc[:, 0:1]))
        fs = fence("scalar")
        em.wait("vector", S["scalar"], fs)
        em.op("vector", lambda e: e.tensor_scalar_add(dt_e[:], dt_e[:], 1.0))
        fv = fence("vector")
        em.wait("scalar", S["vector"], fv)
        em.op("scalar", lambda e: e.activation(dt_f[:], dt_e[:], AF.Ln))
        fs_dt = fence("scalar")
        em.wait("tensor", S["scalar"], fs_dt)
        for c in range(2):
            em.op("tensor", lambda e, c=c: e.transpose(
                pb[0][:, c * 32:(c + 1) * 32], dt_f[0:32, c * 128:(c + 1) * 128],
                ident[0:32, 0:32]))
        ftt = fence("tensor")
        em.wait("scalar", S["tensor"], ftt)
        em.op("scalar", lambda e: e.copy(dtT[:], pb[0][:, 0:64]))
        fs = fence("scalar")
        fs_dtT = fs
        em.wait("vector", S["scalar"], fs)
        em.op("vector", lambda e: e.tensor_tensor(
            adt[:], dtT[:], bass.AP(abc, 0, [[32, 128], [0, 2], [1, 32]]), ALU.mult))
        em.op("vector", lambda e: e.tensor_tensor(
            adtn[:], dtT[:], bass.AP(abcn, 0, [[32, 128], [0, 2], [1, 32]]), ALU.mult))
        fv_dt = fence("vector")
        em.wait("vector", S["vector"], fv_dt) if False else None
        for g in range(18):
            em.op("vector", lambda e, g=g: e.tensor_scalar(
                pf2[:, g * 256:(g + 1) * 256], pf1[:, g * 259: g * 259 + 256],
                convw[:, 4 * g:4 * g + 1], convb[:, g:g + 1],
                ALU.mult, ALU.add))
        for tap in (1, 2, 3):
            _sf = fence("vector")
            em.wait("vector", S["vector"], _sf)
            for g in range(18):
                em.op("vector", lambda e, g=g, t=tap: e.scalar_tensor_tensor(
                    pf2[:, g * 256:(g + 1) * 256], pf1[:, g * 259 + t: g * 259 + t + 256],
                    convw[:, 4 * g + t:4 * g + t + 1], pf2[:, g * 256:(g + 1) * 256],
                    ALU.mult, ALU.add))
        for g in range(18):
            em.op("vector", lambda e, g=g: e.tensor_copy(
                halo[:, g * 3:(g + 1) * 3], pf1[:, g * 259 + 256: g * 259 + 259]))
        fv = fence("vector")
        em.wait("scalar", S["vector"], fv)
        em.op("scalar", lambda e: e.activation(pf2[:, 0:4608], pf2[:, 0:4608], AF.Silu))
        fs_conv = fence("scalar")
        do_taps("conv", tile, {"xbcs": pf2[:, 0:4608], "dtT": dtT[:]})

        for so in range(4):
            p, wsem, wt = wload([("h", wz_hi_d, D_ZX, so * 512, 512, 8, 0),
                           ("l", wz_lo_d, D_ZX, so * 512, 512, 8, 0)])
            em.wait("tensor", wsem, wt)
            for c in range(2):
                bank = pb[4 + (so % 2) * 2 + c]
                for k in range(8):
                    ah = hh_[:, k * 256 + c * 128: k * 256 + (c + 1) * 128]
                    al = hl_[:, k * 256 + c * 128: k * 256 + (c + 1) * 128]
                    wh = wap(p, "h", k * 512, [[1, 512]])
                    wl = wap(p, "l", k * 512, [[1, 512]])
                    em.op("tensor", lambda e, b=bank, a=ah, w=wh, k=k: e.matmul(
                        b[:], a, w, start=(k == 0), stop=False))
                    em.op("tensor", lambda e, b=bank, a=al, w=wh: e.matmul(
                        b[:], a, w, start=False, stop=False))
                    em.op("tensor", lambda e, b=bank, a=ah, w=wl, k=k: e.matmul(
                        b[:], a, w, start=False, stop=(k == 7)))
            ftt = fence("tensor")
            wfree(p, ftt)
            em.wait("scalar", S["tensor"], ftt)
            for c in range(2):
                em.op("scalar", lambda e, so=so, c=c: e.copy(
                    zt[:, c * 2048 + so * 512: c * 2048 + (so + 1) * 512],
                    pb[4 + (so % 2) * 2 + c][:]))
            fs_z = fence("scalar")
            if so >= 1:
                em.wait("tensor", S["scalar"], fs_z)
        do_taps("zx", tile, {"zt": zt[:], "xbc_raw": pf1[:, 0:4608], "dt_f": dt_f[:]})

        em.wait("tensor", S["scalar"], fs_conv)
        em.wait("tensor", S["vector"], fv_dt)
        em.op("tensor", lambda e: e.matmul(
            pb[4][:, 0:64], triu[:], adt[:, 0:64], start=True, stop=False))
        em.op("tensor", lambda e: e.matmul(
            pb[4][:, 64:128], bass.AP(ones_c, 0, [[1, 128], [0, 128]]),
            adt[:, 0:64], start=False, stop=False))
        em.op("tensor", lambda e: e.matmul(
            pb[4][:, 128:192], bass.AP(ones_c, 0, [[1, 128], [0, 128]]),
            adt[:, 0:64], start=False, stop=False))
        em.op("tensor", lambda e: e.matmul(
            pb[4][:, 128:192], triu[:], adtn[:, 0:64], start=False, stop=True))
        ftt = fence("tensor")
        em.wait("scalar", S["tensor"], ftt)
        em.op("scalar", lambda e: e.activation(ddt[:], pb[4][:, 128:192], AF.Exp))
        em.op("scalar", lambda e: e.activation(eacst[:], pb[4][:, 0:64], AF.Exp))
        em.op("scalar", lambda e: e.activation(eab[:], pb[4][:, 64:128], AF.Exp))
        fs = fence("scalar")
        em.wait("vector", S["scalar"], fs)
        em.op("vector", lambda e: e.tensor_tensor(dtd[:], dtT[:], ddt[:], ALU.mult))
        fv_pre = fence("vector")
        em.wait("tensor", S["vector"], fv_pre)
        for c in range(2):
            bq = 16 * 256 + c * 128
            cq = 17 * 256 + c * 128
            em.op("tensor", lambda e, bq=bq, cq=cq: e.matmul(
                pb[5][:, 0:128], pf2[:, bq:bq + 128], pf2[:, cq:cq + 128],
                start=True, stop=True))
            em.op("tensor", lambda e, bq=bq: e.transpose(
                pb[5][:, 128:256], pf2[:, bq:bq + 128], ident[:]))
            ft_g = fence("tensor")
            for kk in range(4):
                for q in range(4):
                    k = kk * 4 + q
                    em.op("tensor", lambda e, k=k, kk=kk, q=q, c=c: e.transpose(
                        pb[kk][:, q * 128:(q + 1) * 128],
                        pf2[:, k * 256 + c * 128: k * 256 + c * 128 + 128],
                        ident[:]))
            ftt = fence("tensor")
            em.wait("vector", S["tensor"], ft_g)
            em.op("vector", lambda e: e.scalar_tensor_tensor(
                gtm[:], pb[5][:, 0:128], 1.0, triu[:], ALU.mult, ALU.mult))
            em.op("vector", lambda e: e.tensor_copy(btsb[:], pb[5][:, 128:256]))
            em.wait("vector", S["tensor"], ftt)
            for kk in range(4):
                em.op("vector", lambda e, kk=kk, c=c: e.tensor_tensor(
                    xdt[:, kk * 512:(kk + 1) * 512], pb[kk][:],
                    bass.AP(dtT, c * 32 + kk * 8, [[64, 128], [1, 8], [0, 64]]),
                    ALU.mult))
            for kk in range(4):
                em.op("vector", lambda e, kk=kk, c=c: e.tensor_tensor(
                    sxdt[:, kk * 512:(kk + 1) * 512], pb[kk][:],
                    bass.AP(dtd, c * 32 + kk * 8, [[64, 128], [1, 8], [0, 64]]),
                    ALU.mult))
            for kk in range(4):
                em.op("vector", lambda e, kk=kk: e.tensor_tensor(
                    pf7[:, kk * 512:(kk + 1) * 512], pb[kk][:],
                    bass.AP(dbc, kk * 8, [[32, 128], [1, 8], [0, 64]]), ALU.mult))
            fv_x = fence("vector")
            em.wait("tensor", S["vector"], fv_x)

            esb_s = [esb, esb2]
            mtsb_s = [mtsb, mtsb2]
            ft_e = {}; fv_min = {}; fs_exp = {}; fv_mt = {}; ft_y = {}; fv_ev = {}

            def emit_E(g):
                eb = pb[6 + (g % 2)]
                if g - 2 >= 0:
                    em.wait("tensor", S["scalar"], fv_min[g - 2])
                for i in range(4):
                    hd = g * 4 + i
                    lhs_a = bass.AP(adt, c * 32 + hd, [[64, 128], [0, 128]])
                    rhs_b = bass.AP(adtn, c * 32 + hd, [[64, 128], [0, 128]])
                    em.op("tensor", lambda e, eb=eb, i=i, a=lhs_a: e.matmul(
                        eb[:, i * 128:(i + 1) * 128], a, triu[:],
                        start=True, stop=False))
                    em.op("tensor", lambda e, eb=eb, i=i, rb=rhs_b: e.matmul(
                        eb[:, i * 128:(i + 1) * 128], triu[:], rb,
                        start=False, stop=True))
                ft_e[g] = fence("tensor")

            def emit_min(g):
                em.wait("scalar", S["tensor"], ft_e[g])
                if g - 2 >= 0:
                    em.wait("scalar", S["gpsimd"], fv_mt[g - 2])
                em.op("scalar", lambda e, g=g: e.activation(
                    esb_s[g % 2][:], pb[6 + (g % 2)][:], AF.Relu, scale=-1.0))
                fv_min[g] = fence("scalar")

            def emit_exp(g):
                em.wait("scalar", S["scalar"], fv_min[g])
                em.op("scalar", lambda e, g=g: e.activation(
                    esb_s[g % 2][:], esb_s[g % 2][:], AF.Exp, scale=-1.0))
                fs_exp[g] = fence("scalar")

            def emit_mt(g):
                em.wait("gpsimd", S["scalar"], fs_exp[g])
                if g - 2 >= 0:
                    em.wait("gpsimd", S["tensor"], ft_y[g - 2])
                em.op("gpsimd", lambda e, g=g: e.tensor_tensor(
                    mtsb_s[g % 2][:], esb_s[g % 2][:],
                    bass.AP(gtm, 0, [[128, 128], [0, 4], [1, 128]]), ALU.mult))
                fv_mt[g] = fence("gpsimd")

            def emit_Y(g):
                yb = pb[4 + (g % 2)]
                em.wait("tensor", S["gpsimd"], fv_mt[g])
                if g - 2 >= 0:
                    em.wait("tensor", S["vector"], fv_ev[g - 2])
                for i in range(4):
                    hd = g * 4 + i
                    em.op("tensor", lambda e, yb=yb, i=i, h=hd, g=g: e.matmul(
                        yb[:, i * 64:(i + 1) * 64], mtsb_s[g % 2][:, i * 128:(i + 1) * 128],
                        xdt[:, h * 64:(h + 1) * 64], start=True, stop=True))
                    em.op("tensor", lambda e, yb=yb, i=i, h=hd, cq=cq: e.matmul(
                        yb[:, 256 + i * 64: 256 + (i + 1) * 64], pf2[:, cq:cq + 128],
                        r_st[:, h * 64:(h + 1) * 64], start=True, stop=True))
                ft_y[g] = fence("tensor")

            fsc_ev = {}

            def emit_evac_s(g):
                yb = pb[4 + (g % 2)]
                em.wait("scalar", S["tensor"], ft_y[g])
                yo = (g % 2) * 256
                for i in range(4):
                    hd = g * 4 + i
                    em.op("scalar", lambda e, yb=yb, i=i, h=hd, c=c, yo=yo: e.activation(
                        yevt[:, yo + i * 64: yo + (i + 1) * 64],
                        yb[:, 256 + i * 64: 256 + (i + 1) * 64], AF.Identity,
                        scale=eacst[:, c * 32 + h: c * 32 + h + 1]))
                fsc_ev[g] = fence("scalar")

            def emit_evac(g):
                yb = pb[4 + (g % 2)]
                yo = (g % 2) * 256
                em.wait("vector", S["scalar"], fsc_ev[g])
                for i in range(4):
                    hd = g * 4 + i
                    em.op("vector", lambda e, yb=yb, i=i, h=hd, c=c, yo=yo: e.tensor_tensor(
                        pf3[:, c * 2048 + h * 64: c * 2048 + (h + 1) * 64],
                        yevt[:, yo + i * 64: yo + (i + 1) * 64],
                        yb[:, i * 64:(i + 1) * 64], ALU.add))
                fv_ev[g] = fence("vector")

            for g in range(10):
                if g < 8:
                    emit_E(g)
                if g == 6:
                    for sl in range(4):
                        em.op("tensor", lambda e, sl=sl: e.matmul(
                            pb[sl][:], btsb[:], sxdt[:, sl * 512:(sl + 1) * 512],
                            start=True, stop=True))
                    ft_s = fence("tensor")
                if g < 8:
                    emit_min(g)
                if g < 8:
                    emit_exp(g)
                if 0 <= g - 1 < 8:
                    emit_mt(g - 1)
                if 0 <= g - 1 < 8:
                    emit_Y(g - 1)
                if 0 <= g - 1 < 8:
                    emit_evac_s(g - 1)
                if 0 <= g - 2 < 8:
                    emit_evac(g - 2)
            _sf = fence("vector")
            em.wait("vector", S["vector"], _sf)
            em.op("vector", lambda e, c=c: e.tensor_tensor(
                pf3[:, c * 2048:(c + 1) * 2048], pf3[:, c * 2048:(c + 1) * 2048],
                pf7[:], ALU.add))
            fv_yd = fence("vector")
            em.wait("vector", S["tensor"], ft_s)
            for sl in range(4):
                em.op("vector", lambda e, sl=sl, c=c: e.tensor_tensor(
                    xts[:, sl * 512:(sl + 1) * 512], r_st[:, sl * 512:(sl + 1) * 512],
                    bass.AP(eab, c * 32 + sl * 8, [[64, 128], [1, 8], [0, 64]]),
                    ALU.mult))
            _sf = fence("vector")
            em.wait("vector", S["vector"], _sf)
            for sl in range(4):
                em.op("vector", lambda e, sl=sl: e.tensor_tensor(
                    r_st[:, sl * 512:(sl + 1) * 512], xts[:, sl * 512:(sl + 1) * 512],
                    pb[sl][:], ALU.add))
            fv_s = fence("vector")
            em.wait("tensor", S["vector"], fv_s)
        do_taps("ssd", tile, {"Y": pf3[:], "r": r_st[:]})

        em.wait("scalar", S["vector"], em.cur(S["vector"]))
        em.op("scalar", lambda e: e.activation(zt[:], zt[:], AF.Silu))
        fs = fence("scalar")
        em.wait("vector", S["scalar"], fs)
        for c in range(2):
            em.op("vector", lambda e, c=c: e.tensor_tensor(
                pf3[:, c * 2048:(c + 1) * 2048], pf3[:, c * 2048:(c + 1) * 2048],
                zt[:, c * 2048:(c + 1) * 2048], ALU.mult))
        _sf = fence("vector")
        em.wait("vector", S["vector"], _sf)
        for c in range(2):
            em.op("vector", lambda e, c=c: e.tensor_tensor(
                pf7[:, c * 1024:(c + 1) * 1024], pf3[:, c * 2048: c * 2048 + 1024],
                pf3[:, c * 2048: c * 2048 + 1024], ALU.mult))
            em.op("vector", lambda e, c=c: e.tensor_tensor(
                xdt[:, c * 1024:(c + 1) * 1024],
                pf3[:, c * 2048 + 1024: c * 2048 + 2048],
                pf3[:, c * 2048 + 1024: c * 2048 + 2048], ALU.mult))
        _sf = fence("vector")
        em.wait("vector", S["vector"], _sf)
        for c in range(2):
            em.op("vector", lambda e, c=c: e.tensor_reduce(
                bass.AP(stats, c, [[16, 128], [1, 1]]),
                bass.AP(pf7, c * 1024, [[2048, 128], [1, 1024]]), AXX, ALU.add))
        for c in range(2):
            em.op("vector", lambda e, c=c: e.tensor_reduce(
                bass.AP(stats, 12 + c, [[16, 128], [1, 1]]),
                bass.AP(xdt, c * 1024, [[2048, 128], [1, 1024]]), AXX, ALU.add))
        _sf = fence("vector")
        em.wait("vector", S["vector"], _sf)
        for c in range(2):
            em.op("vector", lambda e, c=c: e.tensor_tensor(
                stats[:, c:c + 1], stats[:, c:c + 1],
                stats[:, 12 + c:13 + c], ALU.add))
        fv = fence("vector")
        em.wait("scalar", S["vector"], fv)
        em.op("scalar", lambda e: e.activation(stats[:, 4:6], stats[:, 0:2], AF.Ln,
                                               bias=epsc[:, 0:1], scale=1.0 / 2048.0))
        _sfs = fence("scalar")
        em.wait("scalar", S["scalar"], _sfs)
        em.op("scalar", lambda e: e.activation(stats[:, 6:8], stats[:, 4:6], AF.Exp,
                                               scale=-0.5))
        fs = fence("scalar")
        em.wait("vector", S["scalar"], fs)
        for half in range(2):
            for c in range(2):
                em.op("vector", lambda e, c=c, hf=half: e.tensor_scalar(
                    pf7[:, c * 1024:(c + 1) * 1024],
                    pf3[:, c * 2048 + hf * 1024: c * 2048 + (hf + 1) * 1024],
                    stats[:, 6 + c:7 + c], None, ALU.mult))
            fv = fence("vector")
            em.wait("tensor", S["vector"], fv)

            def evac_g(k, kk, phase, half=half):
                kg = half * 8 + k
                if phase == 0:
                    em.op("scalar", lambda e, kg=kg, kk=kk: e.activation(
                        ghl[:, kg * 256:(kg + 1) * 256], pb[kk][:, 0:256],
                        AF.Identity, scale=ngw[:, kg:kg + 1]))
                else:
                    em.op("vector", lambda e, kg=kg, kk=kk: e.scalar_tensor_tensor(
                        ghl[:, 4096 + kg * 256: 4096 + (kg + 1) * 256], pb[kk][:, 0:256],
                        ngw[:, kg:kg + 1], ghl[:, kg * 256:(kg + 1) * 256],
                        ALU.mult, ALU.subtract))
            transpose_1024_block(pf7, evac_g)
        do_taps("gated", tile, {"ghl": None})

        em.wait("gpsimd", S["vector"], em.cur(S["vector"]))
        tx2 = None
        for c in range(2):
            tx2 = dma("gpsimd", DX, pf3[:, c * 1024:(c + 1) * 1024],
                      bass.AP(x_in, (trow + c * 128) * D, [[D, 128], [1, D]]))
        em.wait("vector", DX, tx2)
        em.op("vector", lambda e: e.tensor_tensor(
            outt[:], pf3[:, 0:2048], qt[:], ALU.add))
        fv = fence("vector")
        if tile + 1 < n_tiles:
            fv_ln = t0_load_ln(tile + 1)
        for so in range(2):
            for ks in range(2):
                p, wsem, wt = wload([("h", wo_hi_d, D, so * 512, 512, 8, ks * 8),
                               ("l", wo_lo_d, D, so * 512, 512, 8, ks * 8)])
                em.wait("tensor", wsem, wt)
                if so == 0 and ks == 0:
                    em.wait("tensor", S["vector"], fv)
                for c in range(2):
                    bank = pb[4 + 2 * so + c]
                    for kk in range(8):
                        kg = ks * 8 + kk
                        ah = ghl[:, kg * 256 + c * 128: kg * 256 + (c + 1) * 128]
                        al = bass.AP(ghl, 4096 + kg * 256 + c * 128,
                                     [[8192, 128], [1, 128]])
                        wh = wap(p, "h", kk * 512, [[1, 512]])
                        wl = wap(p, "l", kk * 512, [[1, 512]])
                        st = (ks == 0 and kk == 0)
                        sp = (ks == 1 and kk == 7)
                        em.op("tensor", lambda e, b=bank, a=ah, w=wh, st=st: e.matmul(
                            b[:], a, w, start=st, stop=False))
                        em.op("tensor", lambda e, b=bank, a=al, w=wh: e.matmul(
                            b[:], a, w, start=False, stop=False))
                        em.op("tensor", lambda e, b=bank, a=ah, w=wl, sp=sp: e.matmul(
                            b[:], a, w, start=False, stop=sp))
                ftt = fence("tensor")
                wfree(p, ftt)
        em.wait("vector", S["tensor"], ftt)
        for so in range(2):
            for c in range(2):
                em.op("vector", lambda e, so=so, c=c: e.tensor_tensor(
                    outt[:, c * 1024 + so * 512: c * 1024 + (so + 1) * 512],
                    outt[:, c * 1024 + so * 512: c * 1024 + (so + 1) * 512],
                    pb[4 + 2 * so + c][:], ALU.add))
        fv_wo = fence("vector")
        do_taps("wout", tile, {"outt": outt[:]})

        em.wait("tensor", S["vector"], fv_wo)
        if tile == 0:
            em.wait("sync", DGI[0], int_tok[0])
            em.wait("sync", DGI[1], int_tok[1])
        fv_prod = 0
        for s in range(6):
            p, wsem, wt = wload([("h", wg_b_d, HIDDEN, s * 512, 512, 8, 0),
                           ("l", wu_b_d, HIDDEN, s * 512, 512, 8, 0)])
            em.wait("tensor", wsem, wt)
            if s >= 1:
                em.wait("tensor", S["vector"], fv_prod)
            for m in range(4):
                bank = pb[m]
                for k in range(8):
                    wgt = wap(p, "h", k * 512 + m * 128, [[1, 128]])
                    em.op("tensor", lambda e, b=bank, w=wgt, k=k: e.matmul(
                        b[:, 0:256], w, xnh[:, k * 256:(k + 1) * 256],
                        start=(k == 0), stop=False))
                    em.op("tensor", lambda e, b=bank, w=wgt, k=k: e.matmul(
                        b[:, 0:256], w, xnl[:, k * 256:(k + 1) * 256],
                        start=False, stop=(k == 7)))
            ft_gate = fence("tensor")
            em.wait("vector", S["tensor"], ft_gate)
            for m in range(4):
                em.op("vector", lambda e, m=m, b=pb[m]: e.tensor_scalar(
                    pf1[:, m * 256:(m + 1) * 256], b[:, 0:256],
                    C127_LO, None, ALU.mult))
            for m in range(4):
                em.op("vector", lambda e, m=m, b=pb[m]: e.tensor_copy(
                    pf1[:, 2048 + m * 256: 2048 + (m + 1) * 256], b[:, 0:256]))
            fv_gate = fence("vector")
            em.wait("vector", S["vector"], fv_gate)
            for m in range(4):
                em.op("vector", lambda e, m=m: e.scalar_tensor_tensor(
                    pf1[:, 1024 + m * 256: 1024 + (m + 1) * 256],
                    pf1[:, 2048 + m * 256: 2048 + (m + 1) * 256], C127_HI,
                    pf1[:, m * 256:(m + 1) * 256], ALU.mult, ALU.add))
            fv = fence("vector")
            em.wait("scalar", S["vector"], fv)
            em.op("scalar", lambda e: e.activation(
                pf1[:, 0:1024], pf1[:, 1024:2048], AF.Silu))
            fs = fence("scalar")
            em.wait("tensor", S["vector"], fv_gate)
            for m in range(4):
                bank = pb[m]
                for k in range(8):
                    wut = wap(p, "l", k * 512 + m * 128, [[1, 128]])
                    em.op("tensor", lambda e, b=bank, w=wut, k=k: e.matmul(
                        b[:, 0:256], w, xnh[:, k * 256:(k + 1) * 256],
                        start=(k == 0), stop=False))
                    em.op("tensor", lambda e, b=bank, w=wut, k=k: e.matmul(
                        b[:, 0:256], w, xnl[:, k * 256:(k + 1) * 256],
                        start=False, stop=(k == 7)))
            ft_up = fence("tensor")
            em.wait("vector", S["tensor"], ft_up)
            em.wait("vector", S["scalar"], fs)
            for m in range(4):
                em.op("vector", lambda e, m=m, b=pb[m]: e.tensor_tensor(
                    pf2[:, m * 256:(m + 1) * 256], b[:, 0:256],
                    pf1[:, m * 256:(m + 1) * 256], ALU.mult))
            fv_prod = fence("vector")
            em.wait("scalar", S["vector"], fv_prod)
            for m in range(4):
                em.op("scalar", lambda e, m=m: e.copy(
                    ghl[:, m * 256:(m + 1) * 256], pf2[:, m * 256:(m + 1) * 256]))
            fs2 = fence("scalar")
            em.wait("vector", S["scalar"], fs2)
            for m in range(4):
                em.op("vector", lambda e, m=m: e.scalar_tensor_tensor(
                    ghl[:, 1024 + m * 256: 1024 + (m + 1) * 256],
                    pf2[:, m * 256:(m + 1) * 256], 1.0,
                    ghl[:, m * 256:(m + 1) * 256], ALU.mult, ALU.subtract))
            fv_hf = fence("vector")
            wfree(p, em.cur(S["tensor"]))
            p2, wsem2, wt = wload([("h", wd_b_d, D, 0, 1024, 4, s * 4)])
            em.wait("tensor", wsem2, wt)
            em.wait("tensor", S["vector"], fv_hf)
            for c in range(2):
                for os_ in range(2):
                    bank = pb[4 + 2 * os_ + c]
                    for k in range(4):
                        wdt = wap(p2, "h", k * 1024 + os_ * 512, [[1, 512]])
                        ah = ghl[:, k * 256 + c * 128: k * 256 + (c + 1) * 128]
                        al = ghl[:, 1024 + k * 256 + c * 128:
                                 1024 + k * 256 + (c + 1) * 128]
                        st = (s == 0 and k == 0)
                        sp = (s == 5 and k == 3)
                        em.op("tensor", lambda e, b=bank, a=ah, w=wdt, st=st: e.matmul(
                            b[:], a, w, start=st, stop=False))
                        em.op("tensor", lambda e, b=bank, a=al, w=wdt, sp=sp: e.matmul(
                            b[:], a, w, start=False, stop=sp))
            ftt = fence("tensor")
            wfree(p2, ftt)
        em.wait("vector", S["tensor"], ftt)
        for j, (c, os_) in enumerate([(c, o) for c in range(2) for o in range(2)]):
            bank = pb[4 + 2 * os_ + c]
            em.op("vector", lambda e, b=bank, j=j: e.tensor_scalar(
                pf2[:, j * 512:(j + 1) * 512], b[:], C2_LO, None, ALU.mult))
        _sf = fence("vector")
        em.wait("vector", S["vector"], _sf)
        for j, (c, os_) in enumerate([(c, o) for c in range(2) for o in range(2)]):
            bank = pb[4 + 2 * os_ + c]
            em.op("vector", lambda e, b=bank, j=j: e.scalar_tensor_tensor(
                pf2[:, 2048 + j * 512: 2048 + (j + 1) * 512], b[:], C2_HI,
                pf2[:, j * 512:(j + 1) * 512], ALU.mult, ALU.add))
        _sf = fence("vector")
        em.wait("vector", S["vector"], _sf)
        for j, (c, os_) in enumerate([(c, o) for c in range(2) for o in range(2)]):
            em.op("vector", lambda e, c=c, os_=os_, j=j: e.tensor_tensor(
                outt[:, c * 1024 + os_ * 512: c * 1024 + (os_ + 1) * 512],
                outt[:, c * 1024 + os_ * 512: c * 1024 + (os_ + 1) * 512],
                pf2[:, 2048 + j * 512: 2048 + (j + 1) * 512], ALU.add))
        fv_out = fence("vector")
        do_taps("ffn", tile, {"outt": outt[:]})

        em.wait("gpsimd", S["vector"], fv_out)
        for c in range(2):
            dma("gpsimd", DX, bass.AP(out_d, (trow + c * 128) * D, [[D, 128], [1, D]]),
                outt[:, c * 1024:(c + 1) * 1024])

    em.wait("gpsimd", DG, em.cur(DG))
    em.wait("gpsimd", DX, em.cur(DX))

    import contextlib
    with contextlib.ExitStack() as ctx:
        sems = {name: ctx.enter_context(nc.semaphore(name)) for name in em.sem_names}
        block = ctx.enter_context(nc.Block())

        def run_stream(stream):
            def go(eng_obj):
                for entry in stream:
                    if "wait" in entry:
                        s, v = entry["wait"]
                        eng_obj.wait_ge(sems[s], v)
                    else:
                        inst = entry["fn"](eng_obj)
                        for (s, n) in entry["incs"]:
                            inst.then_inc(sems[s], n)
            return go

        block.sync(run_stream(em.streams["sync"]))
        block.gpsimd(run_stream(em.streams["gpsimd"]))
        block.vector(run_stream(em.streams["vector"]))
        block.scalar(run_stream(em.streams["scalar"]))
        block.tensor(run_stream(em.streams["tensor"]))
    return nc


def stage_inputs(inputs):
    cos, sin = _rope_tables()
    f32c = np.ascontiguousarray
    shared = {
        "wq_t": f32c(np.asarray(inputs["L0_attn_W_in"])[:1024].T),
        "wz_t": f32c(np.asarray(inputs["L0_attn_W_zxbcdt"]).T),
        "wo_t": f32c(np.asarray(inputs["L0_attn_W_out"]).T),
        "wg_t": f32c(np.asarray(inputs["L0_ffn_w_gate"]).T),
        "wu_t": f32c(np.asarray(inputs["L0_ffn_w_up"]).T),
        "wd_t": f32c(np.asarray(inputs["L0_ffn_w_down"]).T),
        "conv_w": f32c(np.asarray(inputs["L0_attn_conv_w"]).reshape(18, 128, 4).transpose(1, 0, 2).reshape(128, 72)),
        "conv_b": f32c(np.asarray(inputs["L0_attn_conv_b"]).reshape(18, 128).T),
        "dt_bias": f32c(np.asarray(inputs["L0_attn_dt_bias"]).reshape(H_SSM, 1)),
        "a_log": f32c(np.asarray(inputs["L0_attn_A_log"]).reshape(1, H_SSM)),
        "d_par": f32c(np.asarray(inputs["L0_attn_D_param"]).reshape(1, H_SSM)),
        "npw": f32c(np.asarray(inputs["L0_attn_norm_pre_w"]).reshape(8, 128).T),
        "ngw": f32c(np.asarray(inputs["L0_attn_norm_gated_w"]).reshape(16, 128).T),
        "triu": f32c(np.triu(np.ones((128, 128), np.float32))),
        "ident": f32c(np.eye(128, dtype=np.float32)),
        "ones_c": np.ones((128, 1), np.float32),
        "ones_r": np.ones((1, 128), np.float32),
        "cos_t": cos,
        "sin_t": sin,
    }
    x = np.asarray(inputs["L0_attn_x"])
    return [dict(shared, x=np.ascontiguousarray(x[b])) for b in range(8)]


def kernel(**inputs):
    in_maps = stage_inputs(inputs)
    nc = build_program()
    res = run_bass_kernel_spmd(nc, in_maps, core_ids=list(range(8)))
    return np.stack([r["out"] for r in res.results]).astype(np.float32)


# version 24 (final)
# speedup vs baseline: 1.0660x; 1.0395x over previous
import numpy as np
import concourse.bass as bass
import concourse.mybir as mybir
from concourse.bass_utils import run_bass_kernel_spmd

F32 = mybir.dt.float32
BF16 = mybir.dt.bfloat16
I32 = mybir.dt.int32
AF = mybir.ActivationFunctionType
ALU = mybir.AluOpType
AXX = mybir.AxisListType.X

T_SEQ = 4096
T_TILE = 256
D = 1024
D_INNER = 2048
CONV_DIM = 2304
D_ZX = 4384
H_SSM = 32
HIDDEN = 3072

EPS = float(np.float32(1e-5))
C127_HI = float(np.float32(1.0 / 127.0))
C127_LO = float(np.float32(1.0 / 127.0 - np.float64(np.float32(1.0 / 127.0))))
C2 = 1.0 / (127.0 * 127.0)
C2_HI = float(np.float32(C2))
C2_LO = float(np.float32(C2 - np.float64(np.float32(C2))))

ENGINES = ("sync", "gpsimd", "vector", "scalar", "tensor")


def _rope_tables():
    import jax
    import jax.numpy as jnp
    cpu = jax.devices("cpu")[0]
    with jax.default_device(cpu):
        inv_freq = 1.0 / 10000.0 ** (jnp.arange(0, 64, 2, dtype=jnp.float32) / 64)
        freqs = jnp.arange(T_SEQ, dtype=jnp.float32)[:, None] * inv_freq[None, :]
        cos = np.asarray(jnp.cos(freqs), dtype=np.float32)
        sin = np.asarray(jnp.sin(freqs), dtype=np.float32)
    return np.ascontiguousarray(cos), np.ascontiguousarray(sin)


class Em:

    def __init__(self):
        self.streams = {e: [] for e in ENGINES}
        self.planned = {}
        self.sem_names = []

    def sem(self, name):
        if name not in self.planned:
            self.planned[name] = 0
            self.sem_names.append(name)
        return name

    def op(self, eng, fn):
        self.streams[eng].append({"fn": fn, "incs": []})

    def inc(self, eng, sem, n=1):
        self.sem(sem)
        assert self.streams[eng] and "fn" in self.streams[eng][-1]
        for (s0, _) in self.streams[eng][-1]["incs"]:
            if s0 == sem:
                return self.planned[sem]
        self.streams[eng][-1]["incs"].append((sem, n))
        self.planned[sem] += n
        return self.planned[sem]

    def wait(self, eng, sem, val):
        self.sem(sem)
        if val > 0:
            self.streams[eng].append({"wait": (sem, val)})

    def cur(self, sem):
        return self.planned.get(sem, 0)


def build_program(n_tiles=16, debug_taps=()):
    nc = bass.Bass(target_bir_lowering=False, trn_type="TRN2")

    x_in = nc.dram_tensor("x", [T_SEQ, D], F32, kind="ExternalInput")
    out_d = nc.dram_tensor("out", [T_SEQ, D], F32, kind="ExternalOutput")

    wq_td = nc.dram_tensor("wq_t", [D, D], F32, kind="ExternalInput")
    wz_td = nc.dram_tensor("wz_t", [D, D_ZX], F32, kind="ExternalInput")
    wo_td = nc.dram_tensor("wo_t", [D_INNER, D], F32, kind="ExternalInput")
    wg_td = nc.dram_tensor("wg_t", [D, HIDDEN], I32, kind="ExternalInput")
    wu_td = nc.dram_tensor("wu_t", [D, HIDDEN], I32, kind="ExternalInput")
    wd_td = nc.dram_tensor("wd_t", [HIDDEN, D], I32, kind="ExternalInput")
    convw_d = nc.dram_tensor("conv_w", [128, 72], F32, kind="ExternalInput")
    convb_d = nc.dram_tensor("conv_b", [128, 18], F32, kind="ExternalInput")
    dtb_d = nc.dram_tensor("dt_bias", [H_SSM, 1], F32, kind="ExternalInput")
    alog_d = nc.dram_tensor("a_log", [1, H_SSM], F32, kind="ExternalInput")
    dpar_d = nc.dram_tensor("d_par", [1, H_SSM], F32, kind="ExternalInput")
    npw_d = nc.dram_tensor("npw", [128, 8], F32, kind="ExternalInput")
    ngw_d = nc.dram_tensor("ngw", [128, 16], F32, kind="ExternalInput")
    triu_d = nc.dram_tensor("triu", [128, 128], F32, kind="ExternalInput")
    ident_d = nc.dram_tensor("ident", [128, 128], F32, kind="ExternalInput")
    onesc_d = nc.dram_tensor("ones_c", [128, 1], F32, kind="ExternalInput")
    ones_d = nc.dram_tensor("ones_r", [1, 128], F32, kind="ExternalInput")
    cos_d = nc.dram_tensor("cos_t", [T_SEQ, 32], F32, kind="ExternalInput")
    sin_d = nc.dram_tensor("sin_t", [T_SEQ, 32], F32, kind="ExternalInput")

    wg_b_d = nc.dram_tensor("wg_b", [D, HIDDEN], BF16, kind="Internal")
    wu_b_d = nc.dram_tensor("wu_b", [D, HIDDEN], BF16, kind="Internal")
    wd_b_d = nc.dram_tensor("wd_b", [HIDDEN, D], BF16, kind="Internal")
    wq_hi_d = nc.dram_tensor("wq_hi", [D, D], BF16, kind="Internal")
    wq_lo_d = nc.dram_tensor("wq_lo", [D, D], BF16, kind="Internal")
    wz_hi_d = nc.dram_tensor("wz_hi", [D, D_ZX], BF16, kind="Internal")
    wz_lo_d = nc.dram_tensor("wz_lo", [D, D_ZX], BF16, kind="Internal")
    wo_hi_d = nc.dram_tensor("wo_hi", [D_INNER, D], BF16, kind="Internal")
    wo_lo_d = nc.dram_tensor("wo_lo", [D_INNER, D], BF16, kind="Internal")

    dbg_ds = [nc.dram_tensor(f"dbg{i}", list(shape), F32, kind="ExternalOutput")
              for i, (_, _, shape) in enumerate(debug_taps)]

    A = nc.alloc_sbuf_tensor
    triu = A("c_triu", [128, 128], F32)
    ident = A("c_ident", [128, 128], F32)
    ones_c = A("c_onesc", [128, 1], F32)
    ones_r = A("c_onesr", [1, 128], F32)
    cosb = A("c_cos", [128, 64], F32)
    sinb = A("c_sin", [128, 64], F32)
    abc = A("c_abc", [128, 32], F32)
    abcn = A("c_abcn", [128, 32], F32)
    dbc = A("c_dbc", [128, 32], F32)
    arow = A("c_arow", [1, 32], F32)
    drow = A("c_drow", [1, 32], F32)
    dtbc = A("c_dtb", [32, 1], F32)
    convw = A("c_convw", [128, 72], F32)
    convb = A("c_convb", [128, 18], F32)
    npw = A("c_npw", [128, 8], F32)
    ngw = A("c_ngw", [128, 16], F32)
    halo = A("b_halo", [128, 54], F32)
    epsc = A("c_eps", [128, 1], F32)

    dt_f = A("b_dtf", [32, 256], F32)
    dt_e = A("b_dte", [32, 256], F32)
    dtT = A("b_dtT", [128, 64], F32)
    adt = A("b_adt", [128, 64], F32)
    adtn = A("b_adtn", [128, 64], F32)
    acst = A("b_acst", [128, 64], F32)
    tmp32 = A("b_tmp32", [128, 64], F32)
    ddt = A("b_ddt", [128, 64], F32)
    eacst = A("b_eacst", [128, 64], F32)
    eab = A("b_eab", [128, 64], F32)
    dtd = A("b_dtd", [128, 64], F32)
    stats = A("b_stats", [128, 16], F32)
    esb = A("b_esb", [128, 512], F32)
    mtsb = A("b_mtsb", [128, 512], F32)
    gtm = A("b_gtm", [128, 128], F32)
    btsb = A("b_btsb", [128, 128], F32)

    qt = A("b_qt", [128, 2048], F32)
    xnh = A("b_xnh", [128, 2048], BF16)
    xnl = A("b_xnl", [128, 2048], BF16)
    hh_ = A("b_hh", [128, 2048], BF16)
    hl_ = A("b_hl", [128, 2048], BF16)
    zt = A("b_zt", [128, 4096], F32)
    pf1 = A("b_pf1", [128, 4662], F32)
    pf2 = A("b_pf2", [128, 4608], F32)
    pf3 = A("b_pf3", [128, 4096], F32)
    pf7 = A("b_pf7", [128, 2048], F32)
    xts = A("b_xts", [128, 2048], F32)
    xdt = A("b_xdt", [128, 2048], F32)
    sxdt = A("b_sxdt", [128, 2048], F32)
    r_st = A("b_r", [128, 2048], F32)
    ghl = A("b_ghl", [128, 8192], BF16)
    esb2 = A("b_esb2", [128, 512], F32)
    mtsb2 = A("b_mtsb2", [128, 512], F32)
    yevt = A("b_yevt", [128, 512], F32)
    outt = A("b_out", [128, 2048], F32)
    wbh = A("b_wbh", [128, 4096], BF16)
    wbl = A("b_wbl", [128, 4096], BF16)
    wbh2 = A("b_wbh2", [128, 4096], BF16)
    wbl2 = A("b_wbl2", [128, 4096], BF16)

    pb = [nc.alloc_psum_tensor(f"pb{i}", [128, 512], F32) for i in range(8)]


    em = Em()
    S = {e: f"S_{e}" for e in ENGINES}
    DQ = "S_dma_sync"
    DG = "S_dma_gps"
    DC_ = "S_dma_const"
    DA = "S_dma_aux"
    DW = ["S_dma_w0", "S_dma_w1"]
    DX = "S_dma_x"

    def dma(eng_q, sem, out, in_):
        em.op(eng_q, lambda e, o=out, i=in_: e.dma_start(out=o, in_=i))
        return em.inc(eng_q, sem, 16)

    def fence(eng):
        return em.inc(eng, S[eng], 1)

    dma("sync", DC_, triu[:], triu_d.ap())
    dma("sync", DC_, ident[:], ident_d.ap())
    dma("sync", DC_, ones_c[:], onesc_d.ap())
    dma("sync", DC_, ones_r[:], ones_d.ap())
    dma("sync", DC_, arow[:], alog_d.ap())
    dma("sync", DC_, drow[:], dpar_d.ap())
    dma("sync", DC_, dtbc[:], dtb_d.ap())
    dma("sync", DC_, convw[:], convw_d.ap())
    dma("sync", DC_, convb[:], convb_d.ap())
    dma("sync", DC_, npw[:], npw_d.ap())
    ld0 = dma("sync", DC_, ngw[:], ngw_d.ap())

    em.wait("scalar", DC_, ld0)
    em.op("scalar", lambda e: e.activation(arow[:], arow[:], AF.Exp))
    fs = fence("scalar")
    em.wait("vector", S["scalar"], fs)
    em.op("vector", lambda e: e.tensor_scalar_mul(arow[:], arow[:], -1.0))
    em.op("vector", lambda e: e.memset(halo[:], 0.0))
    em.op("vector", lambda e: e.memset(epsc[:], EPS))
    em.op("vector", lambda e: e.memset(r_st[:], 0.0))
    fv = fence("vector")
    em.wait("tensor", S["vector"], fv)
    em.op("tensor", lambda e: e.matmul(pb[0][:, 0:32], ones_r[:], arow[:],
                                       start=True, stop=True))
    em.op("tensor", lambda e: e.matmul(pb[0][:, 32:64], ones_r[:], drow[:],
                                       start=True, stop=True))
    ft = fence("tensor")
    em.wait("vector", S["tensor"], ft)
    em.op("vector", lambda e: e.tensor_copy(abc[:], pb[0][:, 0:32]))
    em.op("vector", lambda e: e.tensor_copy(dbc[:], pb[0][:, 32:64]))
    em.op("vector", lambda e: e.tensor_scalar(abcn[:], pb[0][:, 0:32], -1.0, None, ALU.mult))
    fence("vector")

    dsem = ["S_dma_prepg0", "S_dma_prepg1"]

    def ghq(pi, w, lo=False):
        return bass.AP(ghl, pi * 2048 + (4096 if lo else 0),
                       [[8192, 128], [1, w]])

    def prep_fp32(src_d, hi_d, lo_d, K, Dout):
        pieces = [(i * 2048, min(2048, Dout - i * 2048))
                  for i in range((Dout + 2047) // 2048)]
        stage = [pf1, pf2]
        ssem = ["S_dma_prep0", "S_dma_prep1"]
        fv_done = {}
        pi_ctr = [0]
        st_tok = {0: 0, 1: 0}
        for k in range(K // 128):
            sb = stage[k % 2]
            if k >= 2:
                em.wait("sync", S["vector"], fv_done[k - 2])
            t = dma("sync", ssem[k % 2], sb[:, 0:Dout],
                    bass.AP(src_d, k * 128 * Dout, [[Dout, 128], [1, Dout]]))
            em.wait("scalar", ssem[k % 2], t)
            for (c0, w) in pieces:
                pi = pi_ctr[0] % 2
                pi_ctr[0] += 1
                em.wait("scalar", dsem[pi], st_tok[pi])
                em.op("scalar", lambda e, sb=sb, c0=c0, w=w, pi=pi: e.copy(
                    ghq(pi, w), sb[:, c0:c0 + w]))
                fs2 = fence("scalar")
                em.wait("vector", S["scalar"], fs2)
                em.wait("vector", dsem[pi], st_tok[pi])
                em.op("vector", lambda e, sb=sb, c0=c0, w=w, pi=pi: e.scalar_tensor_tensor(
                    ghq(pi, w, True), sb[:, c0:c0 + w], 1.0, ghq(pi, w),
                    ALU.mult, ALU.subtract))
                fv2 = fence("vector")
                em.wait("gpsimd", S["vector"], fv2)
                dma("gpsimd", dsem[pi],
                    bass.AP(hi_d, k * 128 * Dout + c0, [[Dout, 128], [1, w]]),
                    ghq(pi, w))
                st_tok[pi] = dma("gpsimd", dsem[pi],
                                 bass.AP(lo_d, k * 128 * Dout + c0,
                                         [[Dout, 128], [1, w]]),
                                 ghq(pi, w, True))
            fv_done[k] = fv2
        for pi in range(2):
            em.wait("sync", dsem[pi], st_tok[pi])
            em.wait("gpsimd", dsem[pi], st_tok[pi])
        return dict(st_tok)

    DGI = [DG, "S_dma_int1"]
    DGL = "S_dma_gl"
    int_last = {0: None, 1: None}
    int_tok = [0, 0]

    def gint(pi, w, second=False):
        return bass.AP(ghl, (4096 + pi * 1024) if second else pi * 2048,
                       [[8192, 128], [1, w]])

    def prep_int(src_d, dst_d, K, Dout):
        w1 = min(2048, Dout)
        w2 = Dout - w1
        for k in range(K // 128):
            pi = k % 2
            if int_last[pi] is not None:
                em.wait("gpsimd", DGI[pi], int_last[pi])
            t = dma("gpsimd", DGL, gint(pi, w1),
                    bass.AP(src_d, k * 128 * Dout, [[Dout, 128], [1, w1]]))
            if w2:
                t = dma("gpsimd", DGL, gint(pi, w2, True),
                        bass.AP(src_d, k * 128 * Dout + w1,
                                [[Dout, 128], [1, w2]]))
            em.wait("gpsimd", DGL, t)
            t2 = dma("gpsimd", DGI[pi],
                     bass.AP(dst_d, k * 128 * Dout, [[Dout, 128], [1, w1]]),
                     gint(pi, w1))
            if w2:
                t2 = dma("gpsimd", DGI[pi],
                         bass.AP(dst_d, k * 128 * Dout + w1,
                                 [[Dout, 128], [1, w2]]),
                         gint(pi, w2, True))
            int_last[pi] = t2
            int_tok[pi] = t2

    wpair = [{"ht": wbh, "hb": 0, "lt": wbl, "lb": 0, "stride": 4096,
              "free": 0, "sem": DW[0], "gsem": "S_dma_wg0"},
             {"ht": wbh2, "hb": 0, "lt": wbl2, "lb": 0, "stride": 4096,
              "free": 0, "sem": DW[1], "gsem": "S_dma_wg1"},
             {"ht": ghl, "hb": 0, "lt": ghl, "lb": 4096, "stride": 8192,
              "free": 0, "sem": "S_dma_w2", "gsem": "S_dma_wg2"}]
    widx = [0]

    def wap(p, which, off, rest):
        return bass.AP(p[which + "t"], p[which + "b"] + off,
                       [[p["stride"], 128]] + rest)

    def wload(specs, eng="sync", n=2):
        p = wpair[widx[0] % n]
        widx[0] += 1
        em.wait(eng, S["tensor"], p["free"])
        sem = p["sem"] if eng == "sync" else p["gsem"]
        tok = 0
        for (which, src_d, Dout, col0, width, kchunks, k0) in specs:
            em.op(eng, lambda e, p=p, which=which, src=src_d, Do=Dout,
                  c0=col0, w=width, kc=kchunks, kz=k0: e.dma_start(
                      out=wap(p, which, 0, [[w, kc], [1, w]]),
                      in_=bass.AP(src, (kz * 128) * Do + c0,
                                  [[Do, 128], [128 * Do, kc], [1, w]])))
            tok = em.inc(eng, sem, 16)
        return p, sem, tok

    def wfree(p, ftok):
        p["free"] = ftok

    trope_box = [0]

    def t0_load_ln(tile2):
        trow2 = tile2 * T_TILE
        if tile2 == 0:
            tx = tx0_box[0]
            xsem = "S_dma_x0"
        else:
            em.wait("gpsimd", S["vector"], em.cur(S["vector"]))
            tx = None
            xsem = DX
            for c in range(2):
                tx = dma("gpsimd", DX, pf3[:, c * 1024:(c + 1) * 1024],
                         bass.AP(x_in, (trow2 + c * 128) * D, [[D, 128], [1, D]]))
        dma("sync", DA, bass.AP(cosb, 0, [[64, 128], [32, 2], [1, 32]]),
            bass.AP(cos_d, trow2 * 32, [[32, 128], [32 * 128, 2], [1, 32]]))
        trope2 = dma("sync", DA, bass.AP(sinb, 0, [[64, 128], [32, 2], [1, 32]]),
                     bass.AP(sin_d, trow2 * 32, [[32, 128], [32 * 128, 2], [1, 32]]))
        trope_box[0] = trope2
        em.wait("vector", xsem, tx)
        em.op("vector", lambda e: e.tensor_reduce(
            bass.AP(stats, 0, [[16, 128], [1, 2]]),
            bass.AP(pf3, 0, [[4096, 128], [1024, 2], [1, 1024]]), AXX, ALU.add))
        fv = fence("vector")
        em.wait("scalar", S["vector"], fv)
        em.op("scalar", lambda e: e.activation(stats[:, 8:10], stats[:, 0:2],
                                               AF.Identity, scale=1.0 / 1024.0))
        fs = fence("scalar")
        em.wait("vector", S["scalar"], fs)
        for c in range(2):
            em.op("vector", lambda e, c=c: e.tensor_scalar(
                pf3[:, c * 1024:(c + 1) * 1024], pf3[:, c * 1024:(c + 1) * 1024],
                stats[:, 8 + c:9 + c], None, ALU.subtract))
        _sf = fence("vector")
        em.wait("vector", S["vector"], _sf)
        for c in range(2):
            em.op("vector", lambda e, c=c: e.tensor_tensor(
                pf7[:, c * 1024:(c + 1) * 1024], pf3[:, c * 1024:(c + 1) * 1024],
                pf3[:, c * 1024:(c + 1) * 1024], ALU.mult))
        _sf = fence("vector")
        em.wait("vector", S["vector"], _sf)
        for c in range(2):
            em.op("vector", lambda e, c=c: e.tensor_reduce(
                bass.AP(stats, 2 + c, [[16, 128], [1, 1]]),
                bass.AP(pf7, c * 1024, [[2048, 128], [1, 1024]]), AXX, ALU.add))
        fv = fence("vector")
        em.wait("scalar", S["vector"], fv)
        em.op("scalar", lambda e: e.activation(stats[:, 4:6], stats[:, 2:4], AF.Ln,
                                               bias=epsc[:, 0:1], scale=1.0 / 1024.0))
        _sfs = fence("scalar")
        em.wait("scalar", S["scalar"], _sfs)
        em.op("scalar", lambda e: e.activation(stats[:, 6:8], stats[:, 4:6], AF.Exp,
                                               scale=-0.5))
        fs = fence("scalar")
        em.wait("vector", S["scalar"], fs)
        for c in range(2):
            em.op("vector", lambda e, c=c: e.tensor_scalar(
                pf3[:, c * 1024:(c + 1) * 1024], pf3[:, c * 1024:(c + 1) * 1024],
                stats[:, 6 + c:7 + c], None, ALU.mult))
        return fence("vector")

    tx0_box = [None]
    for c in range(2):
        tx0_box[0] = dma("sync", "S_dma_x0", pf3[:, c * 1024:(c + 1) * 1024],
                         bass.AP(x_in, (c * 128) * D, [[D, 128], [1, D]]))
    prep_fp32(wq_td, wq_hi_d, wq_lo_d, D, D)
    prep_fp32(wz_td, wz_hi_d, wz_lo_d, D, D_ZX)
    prep_fp32(wo_td, wo_hi_d, wo_lo_d, D_INNER, D)
    em.wait("gpsimd", S["scalar"], em.cur(S["scalar"]))
    em.wait("gpsimd", S["vector"], em.cur(S["vector"]))
    prep_int(wg_td, wg_b_d, D, HIDDEN)
    prep_int(wu_td, wu_b_d, D, HIDDEN)
    prep_int(wd_td, wd_b_d, HIDDEN, D)
    em.wait("gpsimd", DGI[0], int_tok[0])
    em.wait("gpsimd", DGI[1], int_tok[1])

    tapmap = {}
    for i, (phase, buf, shape) in enumerate(debug_taps):
        tapmap.setdefault(phase, []).append((i, buf))

    def do_taps(phase, tile, locs):
        if tile != 0 or phase not in tapmap:
            return
        for i, buf in tapmap[phase]:
            em.wait("gpsimd", S["vector"], em.cur(S["vector"]))
            em.wait("gpsimd", S["scalar"], em.cur(S["scalar"]))
            em.wait("gpsimd", S["tensor"], em.cur(S["tensor"]))
            em.wait("gpsimd", DG, em.cur(DG))
            em.wait("gpsimd", DX, em.cur(DX))
            em.wait("gpsimd", DQ, em.cur(DQ))
            em.wait("gpsimd", DC_, em.cur(DC_))
            em.wait("gpsimd", DA, em.cur(DA))
            t = dma("gpsimd", DX, dbg_ds[i].ap(), locs[buf])
            em.wait("gpsimd", DX, t)
            em.wait("vector", DX, t)
            em.wait("scalar", DX, t)
            em.wait("tensor", DX, t)
            em.wait("sync", DX, t)

    def transpose_1024_block(src, evac):
        for rnd in range(2):
            for kk in range(4):
                k = rnd * 4 + kk
                for c in range(2):
                    em.op("tensor", lambda e, k=k, c=c, kk=kk: e.transpose(
                        pb[kk][:, c * 128:(c + 1) * 128],
                        src[:, c * 1024 + k * 128: c * 1024 + (k + 1) * 128],
                        ident[:]))
            ftt = fence("tensor")
            em.wait("scalar", S["tensor"], ftt)
            for kk in range(4):
                evac(rnd * 4 + kk, kk, 0)
            fss = fence("scalar")
            em.wait("vector", S["scalar"], fss)
            for kk in range(4):
                evac(rnd * 4 + kk, kk, 1)
            fvv = fence("vector")
            em.wait("tensor", S["vector"], fvv)
        return fvv

    fv_ln = 0
    for tile in range(n_tiles):
        trow = tile * T_TILE


        if tile == 0:
            fv_ln = t0_load_ln(0)
        fv = fv_ln
        trope = trope_box[0]
        do_taps("ln", tile, {"xn_t": pf3[:, 0:2048]})
        em.wait("tensor", S["vector"], fv)

        def evac_xn(k, kk, phase):
            if phase == 0:
                em.op("scalar", lambda e, k=k, kk=kk: e.copy(
                    xnh[:, k * 256:(k + 1) * 256], pb[kk][:, 0:256]))
            else:
                em.op("vector", lambda e, k=k, kk=kk: e.scalar_tensor_tensor(
                    xnl[:, k * 256:(k + 1) * 256], pb[kk][:, 0:256], 1.0,
                    xnh[:, k * 256:(k + 1) * 256], ALU.mult, ALU.subtract))
        transpose_1024_block(pf3, evac_xn)

        for so in range(2):
            p, wsem, wt = wload([("h", wq_hi_d, D, so * 512, 512, 8, 0),
                           ("l", wq_lo_d, D, so * 512, 512, 8, 0)])
            em.wait("tensor", wsem, wt)
            for c in range(2):
                bank = pb[4 + 2 * so + c]
                for k in range(8):
                    ah = xnh[:, k * 256 + c * 128: k * 256 + (c + 1) * 128]
                    al = xnl[:, k * 256 + c * 128: k * 256 + (c + 1) * 128]
                    wh = wap(p, "h", k * 512, [[1, 512]])
                    wl = wap(p, "l", k * 512, [[1, 512]])
                    em.op("tensor", lambda e, b=bank, a=ah, w=wh, k=k: e.matmul(
                        b[:], a, w, start=(k == 0), stop=False))
                    em.op("tensor", lambda e, b=bank, a=al, w=wh: e.matmul(
                        b[:], a, w, start=False, stop=False))
                    em.op("tensor", lambda e, b=bank, a=ah, w=wl, k=k: e.matmul(
                        b[:], a, w, start=False, stop=(k == 7)))
            ftt = fence("tensor")
            wfree(p, ftt)
        em.wait("scalar", S["tensor"], ftt)
        for so in range(2):
            for c in range(2):
                em.op("scalar", lambda e, so=so, c=c: e.copy(
                    pf7[:, c * 1024 + so * 512: c * 1024 + (so + 1) * 512],
                    pb[4 + 2 * so + c][:]))
        fs_q = fence("scalar")
        em.wait("vector", S["scalar"], fs_q)
        em.wait("vector", DA, trope)
        combines = []
        for c in range(2):
            base = c * 1024
            qr_lo = bass.AP(pf7, base, [[2048, 128], [64, 16], [1, 32]])
            qr_hi = bass.AP(pf7, base + 32, [[2048, 128], [64, 16], [1, 32]])
            cos_ap = bass.AP(cosb, c * 32, [[64, 128], [0, 16], [1, 32]])
            sin_ap = bass.AP(sinb, c * 32, [[64, 128], [0, 16], [1, 32]])
            qt_lo = bass.AP(qt, base, [[2048, 128], [64, 16], [1, 32]])
            qt_hi = bass.AP(qt, base + 32, [[2048, 128], [64, 16], [1, 32]])
            tbuf = xdt if c == 0 else sxdt
            t_ap = lambda j, tb=tbuf: bass.AP(tb, j * 512, [[2048, 128], [32, 16], [1, 32]])
            em.op("vector", lambda e, a=qr_lo, b=cos_ap, o=t_ap(0): e.tensor_tensor(o, a, b, ALU.mult))
            em.op("vector", lambda e, a=qr_hi, b=sin_ap, o=t_ap(1): e.tensor_tensor(o, a, b, ALU.mult))
            em.op("vector", lambda e, a=qr_hi, b=cos_ap, o=t_ap(2): e.tensor_tensor(o, a, b, ALU.mult))
            em.op("vector", lambda e, a=qr_lo, b=sin_ap, o=t_ap(3): e.tensor_tensor(o, a, b, ALU.mult))
            combines.append((qt_lo, t_ap(0), t_ap(1), ALU.subtract))
            combines.append((qt_hi, t_ap(2), t_ap(3), ALU.add))
        _sf = fence("vector")
        em.wait("vector", S["vector"], _sf)
        for (o, a, b, op_) in combines:
            em.op("vector", lambda e, o=o, a=a, b=b, op_=op_: e.tensor_tensor(o, a, b, op_))
        fv = fence("vector")
        do_taps("rope", tile, {"qt": qt[:]})

        _sf = fence("vector")
        em.wait("vector", S["vector"], _sf)
        for c in range(2):
            em.op("vector", lambda e, c=c: e.tensor_tensor(
                pf7[:, c * 1024:(c + 1) * 1024], qt[:, c * 1024:(c + 1) * 1024],
                qt[:, c * 1024:(c + 1) * 1024], ALU.mult))
        _sf = fence("vector")
        em.wait("vector", S["vector"], _sf)
        for c in range(2):
            em.op("vector", lambda e, c=c: e.tensor_reduce(
                bass.AP(stats, c, [[16, 128], [1, 1]]),
                bass.AP(pf7, c * 1024, [[2048, 128], [1, 1024]]), AXX, ALU.add))
        fv = fence("vector")
        em.wait("scalar", S["vector"], fv)
        em.op("scalar", lambda e: e.activation(stats[:, 4:6], stats[:, 0:2], AF.Ln,
                                               bias=epsc[:, 0:1], scale=1.0 / 1024.0))
        _sfs = fence("scalar")
        em.wait("scalar", S["scalar"], _sfs)
        em.op("scalar", lambda e: e.activation(stats[:, 6:8], stats[:, 4:6], AF.Exp,
                                               scale=-0.5))
        fs = fence("scalar")
        em.wait("vector", S["scalar"], fs)
        for c in range(2):
            em.op("vector", lambda e, c=c: e.tensor_scalar(
                pf7[:, c * 1024:(c + 1) * 1024], qt[:, c * 1024:(c + 1) * 1024],
                stats[:, 6 + c:7 + c], None, ALU.mult))
        fv = fence("vector")
        em.wait("tensor", S["vector"], fv)

        def evac_h(k, kk, phase):
            if phase == 0:
                em.op("scalar", lambda e, k=k, kk=kk: e.activation(
                    hh_[:, k * 256:(k + 1) * 256], pb[kk][:, 0:256],
                    AF.Identity, scale=npw[:, k:k + 1]))
            else:
                em.op("vector", lambda e, k=k, kk=kk: e.scalar_tensor_tensor(
                    hl_[:, k * 256:(k + 1) * 256], pb[kk][:, 0:256], npw[:, k:k + 1],
                    hh_[:, k * 256:(k + 1) * 256], ALU.mult, ALU.subtract))
        transpose_1024_block(pf7, evac_h)

        fv_evac = 0
        for sf in range(5):
            col0 = D_INNER + sf * 512
            width = min(512, D_ZX - col0)
            p, wsem, wt = wload([("h", wz_hi_d, D_ZX, col0, width, 8, 0),
                           ("l", wz_lo_d, D_ZX, col0, width, 8, 0)])
            em.wait("tensor", wsem, wt)
            if sf >= 1:
                em.wait("tensor", S["vector"], fv_evac)
            nm = (width + 127) // 128
            for m in range(nm):
                mw = min(128, width - m * 128)
                bank = pb[m % 4]
                for k in range(8):
                    wh = wap(p, "h", k * width + m * 128, [[1, mw]])
                    wl = wap(p, "l", k * width + m * 128, [[1, mw]])
                    rh = hh_[:, k * 256:(k + 1) * 256]
                    rl = hl_[:, k * 256:(k + 1) * 256]
                    em.op("tensor", lambda e, b=bank, w=wh, r=rh, k=k, mw=mw: e.matmul(
                        b[0:mw, 0:256], w, r, start=(k == 0), stop=False))
                    em.op("tensor", lambda e, b=bank, w=wl, r=rh, mw=mw: e.matmul(
                        b[0:mw, 0:256], w, r, start=False, stop=False))
                    em.op("tensor", lambda e, b=bank, w=wh, r=rl, k=k, mw=mw: e.matmul(
                        b[0:mw, 0:256], w, r, start=False, stop=(k == 7)))
            ftt = fence("tensor")
            wfree(p, ftt)
            em.wait("vector", S["tensor"], ftt)
            for m in range(nm):
                g = sf * 4 + m
                bank = pb[m % 4]
                if g < 18:
                    em.op("vector", lambda e, g=g, b=bank: e.tensor_copy(
                        pf1[:, g * 259 + 3: g * 259 + 259], b[:, 0:256]))
                else:
                    em.op("vector", lambda e, b=bank: e.tensor_copy(
                        dt_f[:, 0:256], b[0:32, 0:256]))
            fv_evac = fence("vector")
        em.wait("tensor", S["vector"], fv_evac)
        for g in range(18):
            em.op("vector", lambda e, g=g: e.tensor_copy(
                pf1[:, g * 259: g * 259 + 3], halo[:, g * 3:(g + 1) * 3]))
        fv = fence("vector")
        em.wait("scalar", S["vector"], fv)
        em.op("scalar", lambda e: e.activation(dt_e[:], dt_f[:], AF.Exp, bias=dtbc[:, 0:1]))
        fs = fence("scalar")
        em.wait("vector", S["scalar"], fs)
        em.op("vector", lambda e: e.tensor_scalar_add(dt_e[:], dt_e[:], 1.0))
        fv = fence("vector")
        em.wait("scalar", S["vector"], fv)
        em.op("scalar", lambda e: e.activation(dt_f[:], dt_e[:], AF.Ln))
        fs_dt = fence("scalar")
        em.wait("tensor", S["scalar"], fs_dt)
        for c in range(2):
            em.op("tensor", lambda e, c=c: e.transpose(
                pb[0][:, c * 32:(c + 1) * 32], dt_f[0:32, c * 128:(c + 1) * 128],
                ident[0:32, 0:32]))
        ftt = fence("tensor")
        em.wait("scalar", S["tensor"], ftt)
        em.op("scalar", lambda e: e.copy(dtT[:], pb[0][:, 0:64]))
        fs = fence("scalar")
        fs_dtT = fs
        em.wait("vector", S["scalar"], fs)
        em.op("vector", lambda e: e.tensor_tensor(
            adt[:], dtT[:], bass.AP(abc, 0, [[32, 128], [0, 2], [1, 32]]), ALU.mult))
        em.op("vector", lambda e: e.tensor_tensor(
            adtn[:], dtT[:], bass.AP(abcn, 0, [[32, 128], [0, 2], [1, 32]]), ALU.mult))
        fv_dt = fence("vector")
        em.wait("vector", S["vector"], fv_dt) if False else None
        for g in range(18):
            em.op("vector", lambda e, g=g: e.tensor_scalar(
                pf2[:, g * 256:(g + 1) * 256], pf1[:, g * 259: g * 259 + 256],
                convw[:, 4 * g:4 * g + 1], convb[:, g:g + 1],
                ALU.mult, ALU.add))
        for tap in (1, 2, 3):
            _sf = fence("vector")
            em.wait("vector", S["vector"], _sf)
            for g in range(18):
                em.op("vector", lambda e, g=g, t=tap: e.scalar_tensor_tensor(
                    pf2[:, g * 256:(g + 1) * 256], pf1[:, g * 259 + t: g * 259 + t + 256],
                    convw[:, 4 * g + t:4 * g + t + 1], pf2[:, g * 256:(g + 1) * 256],
                    ALU.mult, ALU.add))
        for g in range(18):
            em.op("vector", lambda e, g=g: e.tensor_copy(
                halo[:, g * 3:(g + 1) * 3], pf1[:, g * 259 + 256: g * 259 + 259]))
        fv = fence("vector")
        em.wait("scalar", S["vector"], fv)
        em.op("scalar", lambda e: e.activation(pf2[:, 0:4608], pf2[:, 0:4608], AF.Silu))
        fs_conv = fence("scalar")
        do_taps("conv", tile, {"xbcs": pf2[:, 0:4608], "dtT": dtT[:]})

        em.wait("tensor", S["scalar"], fs_dtT)
        em.wait("tensor", S["vector"], fv_dt)
        em.op("tensor", lambda e: e.matmul(
            pb[0][:, 0:64], triu[:], adt[:, 0:64], start=True, stop=False))
        em.op("tensor", lambda e: e.matmul(
            pb[0][:, 64:128], bass.AP(ones_c, 0, [[1, 128], [0, 128]]),
            adt[:, 0:64], start=False, stop=False))
        em.op("tensor", lambda e: e.matmul(
            pb[0][:, 128:192], bass.AP(ones_c, 0, [[1, 128], [0, 128]]),
            adt[:, 0:64], start=False, stop=False))
        em.op("tensor", lambda e: e.matmul(
            pb[0][:, 128:192], triu[:], adtn[:, 0:64], start=False, stop=True))
        ftt = fence("tensor")
        em.wait("scalar", S["tensor"], ftt)
        em.op("scalar", lambda e: e.activation(ddt[:], pb[0][:, 128:192], AF.Exp))
        em.op("scalar", lambda e: e.activation(eacst[:], pb[0][:, 0:64], AF.Exp))
        em.op("scalar", lambda e: e.activation(eab[:], pb[0][:, 64:128], AF.Exp))
        fs_pre = fence("scalar")
        em.wait("vector", S["scalar"], fs_pre)
        em.op("vector", lambda e: e.tensor_tensor(dtd[:], dtT[:], ddt[:], ALU.mult))
        fv_pre = fence("vector")

        for so in range(4):
            p, wsem, wt = wload([("h", wz_hi_d, D_ZX, so * 512, 512, 8, 0),
                           ("l", wz_lo_d, D_ZX, so * 512, 512, 8, 0)])
            em.wait("tensor", wsem, wt)
            for c in range(2):
                bank = pb[4 + (so % 2) * 2 + c]
                for k in range(8):
                    ah = hh_[:, k * 256 + c * 128: k * 256 + (c + 1) * 128]
                    al = hl_[:, k * 256 + c * 128: k * 256 + (c + 1) * 128]
                    wh = wap(p, "h", k * 512, [[1, 512]])
                    wl = wap(p, "l", k * 512, [[1, 512]])
                    em.op("tensor", lambda e, b=bank, a=ah, w=wh, k=k: e.matmul(
                        b[:], a, w, start=(k == 0), stop=False))
                    em.op("tensor", lambda e, b=bank, a=al, w=wh: e.matmul(
                        b[:], a, w, start=False, stop=False))
                    em.op("tensor", lambda e, b=bank, a=ah, w=wl, k=k: e.matmul(
                        b[:], a, w, start=False, stop=(k == 7)))
            ftt = fence("tensor")
            wfree(p, ftt)
            em.wait("scalar", S["tensor"], ftt)
            for c in range(2):
                em.op("scalar", lambda e, so=so, c=c: e.copy(
                    zt[:, c * 2048 + so * 512: c * 2048 + (so + 1) * 512],
                    pb[4 + (so % 2) * 2 + c][:]))
            fs_z = fence("scalar")
            if 1 <= so < 3:
                em.wait("tensor", S["scalar"], fs_z)
            fs_z_last = fs_z
        do_taps("zx", tile, {"zt": zt[:], "xbc_raw": pf1[:, 0:4608], "dt_f": dt_f[:]})

        em.wait("tensor", S["scalar"], fs_conv)
        em.wait("tensor", S["vector"], fv_pre)
        for c in range(2):
            bq = 16 * 256 + c * 128
            cq = 17 * 256 + c * 128
            em.op("tensor", lambda e, bq=bq, cq=cq: e.matmul(
                pb[5][:, 0:128], pf2[:, bq:bq + 128], pf2[:, cq:cq + 128],
                start=True, stop=True))
            em.op("tensor", lambda e, bq=bq: e.transpose(
                pb[5][:, 128:256], pf2[:, bq:bq + 128], ident[:]))
            ft_g = fence("tensor")
            for kk in range(4):
                for q in range(4):
                    k = kk * 4 + q
                    em.op("tensor", lambda e, k=k, kk=kk, q=q, c=c: e.transpose(
                        pb[kk][:, q * 128:(q + 1) * 128],
                        pf2[:, k * 256 + c * 128: k * 256 + c * 128 + 128],
                        ident[:]))
            ftt = fence("tensor")
            em.wait("vector", S["tensor"], ft_g)
            em.op("vector", lambda e: e.scalar_tensor_tensor(
                gtm[:], pb[5][:, 0:128], 1.0, triu[:], ALU.mult, ALU.mult))
            em.op("vector", lambda e: e.tensor_copy(btsb[:], pb[5][:, 128:256]))
            em.wait("vector", S["tensor"], ftt)
            for kk in range(4):
                em.op("vector", lambda e, kk=kk, c=c: e.tensor_tensor(
                    xdt[:, kk * 512:(kk + 1) * 512], pb[kk][:],
                    bass.AP(dtT, c * 32 + kk * 8, [[64, 128], [1, 8], [0, 64]]),
                    ALU.mult))
            for kk in range(4):
                em.op("vector", lambda e, kk=kk, c=c: e.tensor_tensor(
                    sxdt[:, kk * 512:(kk + 1) * 512], pb[kk][:],
                    bass.AP(dtd, c * 32 + kk * 8, [[64, 128], [1, 8], [0, 64]]),
                    ALU.mult))
            for kk in range(4):
                em.op("vector", lambda e, kk=kk: e.tensor_tensor(
                    pf7[:, kk * 512:(kk + 1) * 512], pb[kk][:],
                    bass.AP(dbc, kk * 8, [[32, 128], [1, 8], [0, 64]]), ALU.mult))
            fv_x = fence("vector")
            em.wait("tensor", S["vector"], fv_x)

            esb_s = [esb, esb2]
            mtsb_s = [mtsb, mtsb2]
            ft_e = {}; fv_min = {}; fs_exp = {}; fv_mt = {}; ft_y = {}; fv_ev = {}

            def emit_E(g):
                eb = pb[6 + (g % 2)]
                if g - 2 >= 0:
                    em.wait("tensor", S["scalar"], fv_min[g - 2])
                for i in range(4):
                    hd = g * 4 + i
                    lhs_a = bass.AP(adt, c * 32 + hd, [[64, 128], [0, 128]])
                    rhs_b = bass.AP(adtn, c * 32 + hd, [[64, 128], [0, 128]])
                    em.op("tensor", lambda e, eb=eb, i=i, a=lhs_a: e.matmul(
                        eb[:, i * 128:(i + 1) * 128], a, triu[:],
                        start=True, stop=False))
                    em.op("tensor", lambda e, eb=eb, i=i, rb=rhs_b: e.matmul(
                        eb[:, i * 128:(i + 1) * 128], triu[:], rb,
                        start=False, stop=True))
                ft_e[g] = fence("tensor")

            def emit_min(g):
                em.wait("scalar", S["tensor"], ft_e[g])
                if g - 2 >= 0:
                    em.wait("scalar", S["gpsimd"], fv_mt[g - 2])
                em.op("scalar", lambda e, g=g: e.activation(
                    esb_s[g % 2][:], pb[6 + (g % 2)][:], AF.Relu, scale=-1.0))
                fv_min[g] = fence("scalar")

            def emit_exp(g):
                em.wait("scalar", S["scalar"], fv_min[g])
                em.op("scalar", lambda e, g=g: e.activation(
                    esb_s[g % 2][:], esb_s[g % 2][:], AF.Exp, scale=-1.0))
                fs_exp[g] = fence("scalar")

            def emit_mt(g):
                em.wait("gpsimd", S["scalar"], fs_exp[g])
                if g - 2 >= 0:
                    em.wait("gpsimd", S["tensor"], ft_y[g - 2])
                em.op("gpsimd", lambda e, g=g: e.tensor_tensor(
                    mtsb_s[g % 2][:], esb_s[g % 2][:],
                    bass.AP(gtm, 0, [[128, 128], [0, 4], [1, 128]]), ALU.mult))
                fv_mt[g] = fence("gpsimd")

            def emit_Y(g):
                yb = pb[4 + (g % 2)]
                em.wait("tensor", S["gpsimd"], fv_mt[g])
                if g - 2 >= 0:
                    em.wait("tensor", S["vector"], fv_ev[g - 2])
                for i in range(4):
                    hd = g * 4 + i
                    em.op("tensor", lambda e, yb=yb, i=i, h=hd, g=g: e.matmul(
                        yb[:, i * 64:(i + 1) * 64], mtsb_s[g % 2][:, i * 128:(i + 1) * 128],
                        xdt[:, h * 64:(h + 1) * 64], start=True, stop=True))
                    em.op("tensor", lambda e, yb=yb, i=i, h=hd, cq=cq: e.matmul(
                        yb[:, 256 + i * 64: 256 + (i + 1) * 64], pf2[:, cq:cq + 128],
                        r_st[:, h * 64:(h + 1) * 64], start=True, stop=True))
                ft_y[g] = fence("tensor")

            fsc_ev = {}

            def emit_evac_s(g):
                yb = pb[4 + (g % 2)]
                em.wait("scalar", S["tensor"], ft_y[g])
                yo = (g % 2) * 256
                for i in range(4):
                    hd = g * 4 + i
                    em.op("scalar", lambda e, yb=yb, i=i, h=hd, c=c, yo=yo: e.activation(
                        yevt[:, yo + i * 64: yo + (i + 1) * 64],
                        yb[:, 256 + i * 64: 256 + (i + 1) * 64], AF.Identity,
                        scale=eacst[:, c * 32 + h: c * 32 + h + 1]))
                fsc_ev[g] = fence("scalar")

            def emit_evac(g):
                yb = pb[4 + (g % 2)]
                yo = (g % 2) * 256
                em.wait("vector", S["scalar"], fsc_ev[g])
                for i in range(4):
                    hd = g * 4 + i
                    em.op("vector", lambda e, yb=yb, i=i, h=hd, c=c, yo=yo: e.tensor_tensor(
                        pf3[:, c * 2048 + h * 64: c * 2048 + (h + 1) * 64],
                        yevt[:, yo + i * 64: yo + (i + 1) * 64],
                        yb[:, i * 64:(i + 1) * 64], ALU.add))
                fv_ev[g] = fence("vector")

            if c == 0:
                em.wait("tensor", S["scalar"], fs_z_last)
            for g in range(10):
                if g < 8:
                    emit_E(g)
                if g == 6:
                    for sl in range(4):
                        em.op("tensor", lambda e, sl=sl: e.matmul(
                            pb[sl][:], btsb[:], sxdt[:, sl * 512:(sl + 1) * 512],
                            start=True, stop=True))
                    ft_s = fence("tensor")
                if g < 8:
                    emit_min(g)
                if g < 8:
                    emit_exp(g)
                if 0 <= g - 1 < 8:
                    emit_mt(g - 1)
                if 0 <= g - 1 < 8:
                    emit_Y(g - 1)
                if 0 <= g - 1 < 8:
                    emit_evac_s(g - 1)
                if 0 <= g - 2 < 8:
                    emit_evac(g - 2)
            _sf = fence("vector")
            em.wait("vector", S["vector"], _sf)
            em.op("vector", lambda e, c=c: e.tensor_tensor(
                pf3[:, c * 2048:(c + 1) * 2048], pf3[:, c * 2048:(c + 1) * 2048],
                pf7[:], ALU.add))
            fv_yd = fence("vector")
            em.wait("vector", S["tensor"], ft_s)
            for sl in range(4):
                em.op("vector", lambda e, sl=sl, c=c: e.tensor_tensor(
                    xts[:, sl * 512:(sl + 1) * 512], r_st[:, sl * 512:(sl + 1) * 512],
                    bass.AP(eab, c * 32 + sl * 8, [[64, 128], [1, 8], [0, 64]]),
                    ALU.mult))
            _sf = fence("vector")
            em.wait("vector", S["vector"], _sf)
            for sl in range(4):
                em.op("vector", lambda e, sl=sl: e.tensor_tensor(
                    r_st[:, sl * 512:(sl + 1) * 512], xts[:, sl * 512:(sl + 1) * 512],
                    pb[sl][:], ALU.add))
            fv_s = fence("vector")
            em.wait("tensor", S["vector"], fv_s)
        do_taps("ssd", tile, {"Y": pf3[:], "r": r_st[:]})

        em.wait("scalar", S["vector"], em.cur(S["vector"]))
        em.op("scalar", lambda e: e.activation(zt[:], zt[:], AF.Silu))
        fs = fence("scalar")
        em.wait("vector", S["scalar"], fs)
        for c in range(2):
            em.op("vector", lambda e, c=c: e.tensor_tensor(
                pf3[:, c * 2048:(c + 1) * 2048], pf3[:, c * 2048:(c + 1) * 2048],
                zt[:, c * 2048:(c + 1) * 2048], ALU.mult))
        _sf = fence("vector")
        em.wait("vector", S["vector"], _sf)
        for c in range(2):
            em.op("vector", lambda e, c=c: e.tensor_tensor(
                pf7[:, c * 1024:(c + 1) * 1024], pf3[:, c * 2048: c * 2048 + 1024],
                pf3[:, c * 2048: c * 2048 + 1024], ALU.mult))
            em.op("vector", lambda e, c=c: e.tensor_tensor(
                xdt[:, c * 1024:(c + 1) * 1024],
                pf3[:, c * 2048 + 1024: c * 2048 + 2048],
                pf3[:, c * 2048 + 1024: c * 2048 + 2048], ALU.mult))
        _sf = fence("vector")
        em.wait("vector", S["vector"], _sf)
        for c in range(2):
            em.op("vector", lambda e, c=c: e.tensor_reduce(
                bass.AP(stats, c, [[16, 128], [1, 1]]),
                bass.AP(pf7, c * 1024, [[2048, 128], [1, 1024]]), AXX, ALU.add))
        for c in range(2):
            em.op("vector", lambda e, c=c: e.tensor_reduce(
                bass.AP(stats, 12 + c, [[16, 128], [1, 1]]),
                bass.AP(xdt, c * 1024, [[2048, 128], [1, 1024]]), AXX, ALU.add))
        _sf = fence("vector")
        em.wait("vector", S["vector"], _sf)
        for c in range(2):
            em.op("vector", lambda e, c=c: e.tensor_tensor(
                stats[:, c:c + 1], stats[:, c:c + 1],
                stats[:, 12 + c:13 + c], ALU.add))
        fv = fence("vector")
        em.wait("scalar", S["vector"], fv)
        em.op("scalar", lambda e: e.activation(stats[:, 4:6], stats[:, 0:2], AF.Ln,
                                               bias=epsc[:, 0:1], scale=1.0 / 2048.0))
        _sfs = fence("scalar")
        em.wait("scalar", S["scalar"], _sfs)
        em.op("scalar", lambda e: e.activation(stats[:, 6:8], stats[:, 4:6], AF.Exp,
                                               scale=-0.5))
        fs = fence("scalar")
        em.wait("vector", S["scalar"], fs)
        for half in range(2):
            for c in range(2):
                em.op("vector", lambda e, c=c, hf=half: e.tensor_scalar(
                    pf7[:, c * 1024:(c + 1) * 1024],
                    pf3[:, c * 2048 + hf * 1024: c * 2048 + (hf + 1) * 1024],
                    stats[:, 6 + c:7 + c], None, ALU.mult))
            fv = fence("vector")
            em.wait("tensor", S["vector"], fv)

            def evac_g(k, kk, phase, half=half):
                kg = half * 8 + k
                if phase == 0:
                    em.op("scalar", lambda e, kg=kg, kk=kk: e.activation(
                        ghl[:, kg * 256:(kg + 1) * 256], pb[kk][:, 0:256],
                        AF.Identity, scale=ngw[:, kg:kg + 1]))
                else:
                    em.op("vector", lambda e, kg=kg, kk=kk: e.scalar_tensor_tensor(
                        ghl[:, 4096 + kg * 256: 4096 + (kg + 1) * 256], pb[kk][:, 0:256],
                        ngw[:, kg:kg + 1], ghl[:, kg * 256:(kg + 1) * 256],
                        ALU.mult, ALU.subtract))
            transpose_1024_block(pf7, evac_g)
        do_taps("gated", tile, {"ghl": None})

        em.wait("gpsimd", S["vector"], em.cur(S["vector"]))
        tx2 = None
        for c in range(2):
            tx2 = dma("gpsimd", DX, pf3[:, c * 1024:(c + 1) * 1024],
                      bass.AP(x_in, (trow + c * 128) * D, [[D, 128], [1, D]]))
        em.wait("vector", DX, tx2)
        em.op("vector", lambda e: e.tensor_tensor(
            outt[:], pf3[:, 0:2048], qt[:], ALU.add))
        fv = fence("vector")
        if tile + 1 < n_tiles:
            fv_ln = t0_load_ln(tile + 1)
        for so in range(2):
            for ks in range(2):
                p, wsem, wt = wload([("h", wo_hi_d, D, so * 512, 512, 8, ks * 8),
                               ("l", wo_lo_d, D, so * 512, 512, 8, ks * 8)])
                em.wait("tensor", wsem, wt)
                if so == 0 and ks == 0:
                    em.wait("tensor", S["vector"], fv)
                for c in range(2):
                    bank = pb[4 + 2 * so + c]
                    for kk in range(8):
                        kg = ks * 8 + kk
                        ah = ghl[:, kg * 256 + c * 128: kg * 256 + (c + 1) * 128]
                        al = bass.AP(ghl, 4096 + kg * 256 + c * 128,
                                     [[8192, 128], [1, 128]])
                        wh = wap(p, "h", kk * 512, [[1, 512]])
                        wl = wap(p, "l", kk * 512, [[1, 512]])
                        st = (ks == 0 and kk == 0)
                        sp = (ks == 1 and kk == 7)
                        em.op("tensor", lambda e, b=bank, a=ah, w=wh, st=st: e.matmul(
                            b[:], a, w, start=st, stop=False))
                        em.op("tensor", lambda e, b=bank, a=al, w=wh: e.matmul(
                            b[:], a, w, start=False, stop=False))
                        em.op("tensor", lambda e, b=bank, a=ah, w=wl, sp=sp: e.matmul(
                            b[:], a, w, start=False, stop=sp))
                ftt = fence("tensor")
                wfree(p, ftt)
        em.wait("vector", S["tensor"], ftt)
        for so in range(2):
            for c in range(2):
                em.op("vector", lambda e, so=so, c=c: e.tensor_tensor(
                    outt[:, c * 1024 + so * 512: c * 1024 + (so + 1) * 512],
                    outt[:, c * 1024 + so * 512: c * 1024 + (so + 1) * 512],
                    pb[4 + 2 * so + c][:], ALU.add))
        fv_wo = fence("vector")
        do_taps("wout", tile, {"outt": outt[:]})

        em.wait("tensor", S["vector"], fv_wo)
        if tile == 0:
            em.wait("sync", DGI[0], int_tok[0])
            em.wait("sync", DGI[1], int_tok[1])
        fv_prod = 0
        for s in range(6):
            p, wsem, wt = wload([("h", wg_b_d, HIDDEN, s * 512, 512, 8, 0),
                           ("l", wu_b_d, HIDDEN, s * 512, 512, 8, 0)])
            em.wait("tensor", wsem, wt)
            if s >= 1:
                em.wait("tensor", S["vector"], fv_prod)
            for m in range(4):
                bank = pb[m]
                for k in range(8):
                    wgt = wap(p, "h", k * 512 + m * 128, [[1, 128]])
                    em.op("tensor", lambda e, b=bank, w=wgt, k=k: e.matmul(
                        b[:, 0:256], w, xnh[:, k * 256:(k + 1) * 256],
                        start=(k == 0), stop=False))
                    em.op("tensor", lambda e, b=bank, w=wgt, k=k: e.matmul(
                        b[:, 0:256], w, xnl[:, k * 256:(k + 1) * 256],
                        start=False, stop=(k == 7)))
            ft_gate = fence("tensor")
            em.wait("vector", S["tensor"], ft_gate)
            for m in range(4):
                em.op("vector", lambda e, m=m, b=pb[m]: e.tensor_scalar(
                    pf1[:, m * 256:(m + 1) * 256], b[:, 0:256],
                    C127_LO, None, ALU.mult))
            for m in range(4):
                em.op("vector", lambda e, m=m, b=pb[m]: e.tensor_copy(
                    pf1[:, 2048 + m * 256: 2048 + (m + 1) * 256], b[:, 0:256]))
            fv_gate = fence("vector")
            em.wait("vector", S["vector"], fv_gate)
            for m in range(4):
                em.op("vector", lambda e, m=m: e.scalar_tensor_tensor(
                    pf1[:, 1024 + m * 256: 1024 + (m + 1) * 256],
                    pf1[:, 2048 + m * 256: 2048 + (m + 1) * 256], C127_HI,
                    pf1[:, m * 256:(m + 1) * 256], ALU.mult, ALU.add))
            fv = fence("vector")
            em.wait("scalar", S["vector"], fv)
            em.op("scalar", lambda e: e.activation(
                pf1[:, 0:1024], pf1[:, 1024:2048], AF.Silu))
            fs = fence("scalar")
            em.wait("tensor", S["vector"], fv_gate)
            for m in range(4):
                bank = pb[m]
                for k in range(8):
                    wut = wap(p, "l", k * 512 + m * 128, [[1, 128]])
                    em.op("tensor", lambda e, b=bank, w=wut, k=k: e.matmul(
                        b[:, 0:256], w, xnh[:, k * 256:(k + 1) * 256],
                        start=(k == 0), stop=False))
                    em.op("tensor", lambda e, b=bank, w=wut, k=k: e.matmul(
                        b[:, 0:256], w, xnl[:, k * 256:(k + 1) * 256],
                        start=False, stop=(k == 7)))
            ft_up = fence("tensor")
            em.wait("vector", S["tensor"], ft_up)
            em.wait("vector", S["scalar"], fs)
            for m in range(4):
                em.op("vector", lambda e, m=m, b=pb[m]: e.tensor_tensor(
                    pf2[:, m * 256:(m + 1) * 256], b[:, 0:256],
                    pf1[:, m * 256:(m + 1) * 256], ALU.mult))
            fv_prod = fence("vector")
            em.wait("scalar", S["vector"], fv_prod)
            for m in range(4):
                em.op("scalar", lambda e, m=m: e.copy(
                    ghl[:, m * 256:(m + 1) * 256], pf2[:, m * 256:(m + 1) * 256]))
            fs2 = fence("scalar")
            em.wait("vector", S["scalar"], fs2)
            for m in range(4):
                em.op("vector", lambda e, m=m: e.scalar_tensor_tensor(
                    ghl[:, 1024 + m * 256: 1024 + (m + 1) * 256],
                    pf2[:, m * 256:(m + 1) * 256], 1.0,
                    ghl[:, m * 256:(m + 1) * 256], ALU.mult, ALU.subtract))
            fv_hf = fence("vector")
            wfree(p, em.cur(S["tensor"]))
            p2, wsem2, wt = wload([("h", wd_b_d, D, 0, 1024, 4, s * 4)])
            em.wait("tensor", wsem2, wt)
            em.wait("tensor", S["vector"], fv_hf)
            for c in range(2):
                for os_ in range(2):
                    bank = pb[4 + 2 * os_ + c]
                    for k in range(4):
                        wdt = wap(p2, "h", k * 1024 + os_ * 512, [[1, 512]])
                        ah = ghl[:, k * 256 + c * 128: k * 256 + (c + 1) * 128]
                        al = ghl[:, 1024 + k * 256 + c * 128:
                                 1024 + k * 256 + (c + 1) * 128]
                        st = (s == 0 and k == 0)
                        sp = (s == 5 and k == 3)
                        em.op("tensor", lambda e, b=bank, a=ah, w=wdt, st=st: e.matmul(
                            b[:], a, w, start=st, stop=False))
                        em.op("tensor", lambda e, b=bank, a=al, w=wdt, sp=sp: e.matmul(
                            b[:], a, w, start=False, stop=sp))
            ftt = fence("tensor")
            wfree(p2, ftt)
        em.wait("vector", S["tensor"], ftt)
        for j, (c, os_) in enumerate([(c, o) for c in range(2) for o in range(2)]):
            bank = pb[4 + 2 * os_ + c]
            em.op("vector", lambda e, b=bank, j=j: e.tensor_scalar(
                pf2[:, j * 512:(j + 1) * 512], b[:], C2_LO, None, ALU.mult))
        _sf = fence("vector")
        em.wait("vector", S["vector"], _sf)
        for j, (c, os_) in enumerate([(c, o) for c in range(2) for o in range(2)]):
            bank = pb[4 + 2 * os_ + c]
            em.op("vector", lambda e, b=bank, j=j: e.scalar_tensor_tensor(
                pf2[:, 2048 + j * 512: 2048 + (j + 1) * 512], b[:], C2_HI,
                pf2[:, j * 512:(j + 1) * 512], ALU.mult, ALU.add))
        _sf = fence("vector")
        em.wait("vector", S["vector"], _sf)
        for j, (c, os_) in enumerate([(c, o) for c in range(2) for o in range(2)]):
            em.op("vector", lambda e, c=c, os_=os_, j=j: e.tensor_tensor(
                outt[:, c * 1024 + os_ * 512: c * 1024 + (os_ + 1) * 512],
                outt[:, c * 1024 + os_ * 512: c * 1024 + (os_ + 1) * 512],
                pf2[:, 2048 + j * 512: 2048 + (j + 1) * 512], ALU.add))
        fv_out = fence("vector")
        do_taps("ffn", tile, {"outt": outt[:]})

        em.wait("gpsimd", S["vector"], fv_out)
        for c in range(2):
            dma("gpsimd", DX, bass.AP(out_d, (trow + c * 128) * D, [[D, 128], [1, D]]),
                outt[:, c * 1024:(c + 1) * 1024])

    em.wait("gpsimd", DG, em.cur(DG))
    em.wait("gpsimd", DX, em.cur(DX))

    import contextlib
    with contextlib.ExitStack() as ctx:
        sems = {name: ctx.enter_context(nc.semaphore(name)) for name in em.sem_names}
        block = ctx.enter_context(nc.Block())

        def run_stream(stream):
            def go(eng_obj):
                for entry in stream:
                    if "wait" in entry:
                        s, v = entry["wait"]
                        eng_obj.wait_ge(sems[s], v)
                    else:
                        inst = entry["fn"](eng_obj)
                        for (s, n) in entry["incs"]:
                            inst.then_inc(sems[s], n)
            return go

        block.sync(run_stream(em.streams["sync"]))
        block.gpsimd(run_stream(em.streams["gpsimd"]))
        block.vector(run_stream(em.streams["vector"]))
        block.scalar(run_stream(em.streams["scalar"]))
        block.tensor(run_stream(em.streams["tensor"]))
    return nc


def stage_inputs(inputs):
    cos, sin = _rope_tables()
    f32c = np.ascontiguousarray
    shared = {
        "wq_t": f32c(np.asarray(inputs["L0_attn_W_in"])[:1024].T),
        "wz_t": f32c(np.asarray(inputs["L0_attn_W_zxbcdt"]).T),
        "wo_t": f32c(np.asarray(inputs["L0_attn_W_out"]).T),
        "wg_t": f32c(np.asarray(inputs["L0_ffn_w_gate"]).T),
        "wu_t": f32c(np.asarray(inputs["L0_ffn_w_up"]).T),
        "wd_t": f32c(np.asarray(inputs["L0_ffn_w_down"]).T),
        "conv_w": f32c(np.asarray(inputs["L0_attn_conv_w"]).reshape(18, 128, 4).transpose(1, 0, 2).reshape(128, 72)),
        "conv_b": f32c(np.asarray(inputs["L0_attn_conv_b"]).reshape(18, 128).T),
        "dt_bias": f32c(np.asarray(inputs["L0_attn_dt_bias"]).reshape(H_SSM, 1)),
        "a_log": f32c(np.asarray(inputs["L0_attn_A_log"]).reshape(1, H_SSM)),
        "d_par": f32c(np.asarray(inputs["L0_attn_D_param"]).reshape(1, H_SSM)),
        "npw": f32c(np.asarray(inputs["L0_attn_norm_pre_w"]).reshape(8, 128).T),
        "ngw": f32c(np.asarray(inputs["L0_attn_norm_gated_w"]).reshape(16, 128).T),
        "triu": f32c(np.triu(np.ones((128, 128), np.float32))),
        "ident": f32c(np.eye(128, dtype=np.float32)),
        "ones_c": np.ones((128, 1), np.float32),
        "ones_r": np.ones((1, 128), np.float32),
        "cos_t": cos,
        "sin_t": sin,
    }
    x = np.asarray(inputs["L0_attn_x"])
    return [dict(shared, x=np.ascontiguousarray(x[b])) for b in range(8)]


def kernel(**inputs):
    in_maps = stage_inputs(inputs)
    nc = build_program()
    res = run_bass_kernel_spmd(nc, in_maps, core_ids=list(range(8)))
    return np.stack([r["out"] for r in res.results]).astype(np.float32)
